# version 02
# baseline (speedup 1.0000x reference)
import math
import numpy as np
import ml_dtypes
import concourse.bass as bass
import concourse.mybir as mybir
from concourse.bass_utils import run_bass_kernel_spmd

F32 = mybir.dt.float32
BF16 = mybir.dt.bfloat16
AF = mybir.ActivationFunctionType
ALU = mybir.AluOpType
AX = mybir.AxisListType

S = 8192
D = 4096
H = 32
NCORE = 8
T_OWN = 1024
T_PAN = 2048
NPAN = S // T_PAN
OG = 256
NOG = D // OG
EXP_C = 12.0
SM_SCALE = 1.0 / math.sqrt(128.0)
NTA = 32
NTB = 64
EPS = 1e-5


def build(debug=False):
    nc = bass.Bass("TRN2", target_bir_lowering=False)

    dXT = nc.dram_tensor("xT", [D, S], F32, kind="ExternalInput").ap()
    dXO = nc.dram_tensor("xo", [T_OWN, D], F32, kind="ExternalInput").ap()
    dXOT = nc.dram_tensor("xoT", [D, T_OWN], F32, kind="ExternalInput").ap()
    dWQ = nc.dram_tensor("wq", [D, D], F32, kind="ExternalInput").ap()
    dWK = nc.dram_tensor("wk", [D, D], F32, kind="ExternalInput").ap()
    dWV = nc.dram_tensor("wv", [D, D], F32, kind="ExternalInput").ap()
    dWO = nc.dram_tensor("wo", [D, D], F32, kind="ExternalInput").ap()
    dBWT = nc.dram_tensor("bWT", [D, D], F32, kind="ExternalInput").ap()
    dRWT = nc.dram_tensor("rWT", [D, 8], F32, kind="ExternalInput").ap()
    dAall = nc.dram_tensor("Aall", [D, 128], F32, kind="ExternalInput").ap()
    dBall = nc.dram_tensor("Ball", [128, D], F32, kind="ExternalInput").ap()
    dCOSQ = nc.dram_tensor("cosq", [128, T_OWN], BF16, kind="ExternalInput").ap()
    dSINQ = nc.dram_tensor("sinq", [128, T_OWN], BF16, kind="ExternalInput").ap()
    dCOSK = nc.dram_tensor("cosk", [128, S], BF16, kind="ExternalInput").ap()
    dSINK = nc.dram_tensor("sink", [128, S], BF16, kind="ExternalInput").ap()
    dMA = nc.dram_tensor("maskA", [128, NTA, 512], BF16, kind="ExternalInput").ap()
    dMB = nc.dram_tensor("maskB", [128, NTB, 512], BF16, kind="ExternalInput").ap()
    dID = nc.dram_tensor("ident", [128, 128], BF16, kind="ExternalInput").ap()

    dY = nc.dram_tensor("y", [T_OWN, D], F32, kind="ExternalOutput").ap()
    ikind = "ExternalOutput" if debug else "Internal"
    dWKB = nc.dram_tensor("wkb", [D, D], BF16, kind="Internal").ap()
    dWVB = nc.dram_tensor("wvb", [D, D], BF16, kind="Internal").ap()
    dKG = nc.dram_tensor("Kg", [D, S], BF16, kind=ikind).ap()
    dVG = nc.dram_tensor("Vg", [S, D], BF16, kind=ikind).ap()
    dQG = nc.dram_tensor("qg", [D, T_OWN], BF16, kind=ikind).ap()
    dAG = nc.dram_tensor("ag", [D, T_OWN], BF16, kind=ikind).ap()
    dFG = nc.dram_tensor("fg", [T_OWN, D], F32, kind=ikind).ap()
    dSCR = nc.dram_tensor("scr", [256, 1024], F32, kind="Internal").ap()

    A_ = nc.alloc_sbuf_tensor
    c_id = A_("c_id", [128, 128], BF16)
    c_ones_bf = A_("c_ones_bf", [128, 1], BF16)
    c_ones1 = A_("c_ones1", [1, 128], F32)
    c_negC = A_("c_negC", [128, 1], F32)
    c_eps = A_("c_eps", [128, 1], F32)

    arena = A_("arena", [128, 65536], BF16)
    wblk = [A_(f"wblk{i}", [128, 32, OG], BF16) for i in range(2)]
    x2buf = [A_(f"x2buf{i}", [128, 1024], BF16) for i in range(2)]
    mrow_s = A_("mrow_s", [1, 512], F32)
    rrow_s = A_("rrow_s", [1, 512], F32)
    vtmp_s = A_("vtmp_s", [1, 512], F32)
    mcol = A_("mcol", [128, 16], F32)
    rcol = A_("rcol", [128, 16], F32)
    mbc = A_("mbc", [128, T_PAN], BF16)
    rbc = A_("rbc", [128, T_PAN], F32)
    negc_k = A_("negc_k", [128, 32], F32)
    negc_v = A_("negc_v", [128, 32], F32)
    negc_q = A_("negc_q", [128, 32], F32)
    crow = A_("crow", [1, OG], F32)
    cvrow = A_("cvrow", [1, OG], F32)
    cvbc = A_("cvbc", [128, OG], F32)
    kaff = [A_(f"kaff{i}", [128, 1024], BF16) for i in range(2)]
    ksh = [A_(f"ksh{i}", [128, 1024], BF16) for i in range(2)]
    vout = [A_(f"vout{i}", [128, OG], BF16) for i in range(2)]
    cosbuf = A_("cosbuf", [128, 2048], BF16)
    sinbuf = A_("sinbuf", [128, 2048], BF16)
    mo_col = A_("mo_col", [128, 8], F32)
    negmo_col = A_("negmo_col", [128, 8], F32)
    ro_col = A_("ro_col", [128, 8], F32)
    stat_s = A_("stat_s", [128, 1], F32)
    stat_ss = A_("stat_ss", [128, 1], F32)
    stat_v = A_("stat_v", [128, 1], F32)
    b_lg = A_("b_lg", [128, 8], F32)
    b_eq = A_("b_eq", [128, 8], F32)
    b_msk = A_("b_msk", [128, 8], F32)
    b_m1 = A_("b_m1", [128, 1], F32)
    b_negm1 = A_("b_negm1", [128, 1], F32)
    b_m2 = A_("b_m2", [128, 1], F32)
    b_s1 = A_("b_s1", [128, 8], F32)
    b_den = A_("b_den", [128, 1], F32)
    b_winv = A_("b_winv", [128, 1], F32)
    b_wro = A_("b_wro", [128, 8, 8], F32)
    b_midw = A_("b_midw", [128, 128], BF16)
    b_ncr = A_("b_ncr", [1, 128], F32)
    b_ncr_bc = A_("b_ncr_bc", [128, 8], F32)

    P_ = nc.alloc_psum_tensor
    pz = [P_(f"pz{i}", [128, 512], F32) for i in range(2)]
    pst = [P_(f"pst{i}", [128, 512], F32) for i in range(4)]
    pc = P_("pc", [128, 512], F32)
    ptr = P_("ptr", [128, 128], BF16)

    ENGS = ("tensor", "vector", "scalar", "gpsimd", "sync")
    Q = {e: [] for e in ENGS}

    class Sem:
        def __init__(self, name):
            self.h = nc.alloc_semaphore(name)
            self.n = 0
    s_ld = Sem("s_ld")
    s_cast = Sem("s_cast")
    s_wb = Sem("s_wb")
    s_sh = Sem("s_sh")
    s_out = Sem("s_out")
    s_mm = Sem("s_mm")
    s_ve = Sem("s_ve")
    s_ac = Sem("s_ac")
    s_gp = Sem("s_gp")
    s_st = Sem("s_st")
    s_vx = Sem("s_vx")
    s_ax = Sem("s_ax")

    def emit(eng, fn):
        Q[eng].append(fn)

    def dma(eng_q, out, in_, sem, waits=(), transpose=False, slow=False):
        sem.n += 16
        tgt = sem.n
        def fn(e, out=out, in_=in_, waits=tuple(waits)):
            for (s, t) in waits:
                e.wait_ge(s.h, t)
            e.dma_start(out=out, in_=in_, transpose=transpose,
                        allow_slow_non_contiguous=slow).then_inc(sem.h, 16)
        emit(eng_q, fn)
        return tgt

    def chain(eng, chain_sem, final_sem, steps, waits):
        for i, fn_ in enumerate(steps[:-1]):
            op(eng, fn_, chain_sem,
               list(waits) if i == 0 else [(chain_sem, chain_sem.n)])
        return op(eng, steps[-1], final_sem,
                  [(chain_sem, chain_sem.n)] if len(steps) > 1 else list(waits))

    scr_slot = [0]

    def row2col(row_ap, col_ap, waits, pp=128):
        sl = scr_slot[0]; scr_slot[0] += 1
        n = row_ap.shape[-1]
        t1 = dma("sync", dSCR[sl, 0:n], row_ap, s_sh, waits)
        t2 = dma("sync", col_ap,
                 dSCR[sl, 0:n].rearrange("(t p) -> p t", p=pp), s_sh, [(s_sh, t1)],
                 slow=True)
        return t2

    def col2row(col_ap, row_ap, waits, pp=128):
        sl = scr_slot[0]; scr_slot[0] += 1
        n = row_ap.shape[-1]
        t1 = dma("sync", dSCR[sl, 0:n].rearrange("(t p) -> p t", p=pp), col_ap,
                 s_sh, waits, slow=True)
        t2 = dma("sync", row_ap, dSCR[sl, 0:n], s_sh, [(s_sh, t1)])
        return t2

    def op(eng, build_fn, sem=None, waits=(), k=1):
        tgt = None
        if sem is not None:
            sem.n += k
            tgt = sem.n
        def fn(e, waits=tuple(waits)):
            for (s, t) in waits:
                e.wait_ge(s.h, t)
            ins = build_fn(e)
            if sem is not None:
                ins.then_inc(sem.h, k)
        emit(eng, fn)
        return tgt

    def ar_bf(lo, n):
        return arena[:, lo:lo + n]
    def ar_f32(lo, n):
        return arena[:, lo:lo + 2 * n].bitcast(F32)

    op("gpsimd", lambda e: (nc.gpsimd.memset(c_ones_bf[:], 1.0),
                            nc.gpsimd.memset(c_ones1[:], 1.0),
                            nc.gpsimd.memset(c_negC[:], -EXP_C),
                            nc.gpsimd.memset(c_eps[:], EPS))[-1], s_gp)
    t_consts = s_gp.n
    t_id = dma("sync", c_id[:], dID, s_ld)

    xo_buf = [ar_f32(0, 4096), ar_f32(8192, 4096)]
    sqdump = ar_f32(16384, 4096)
    a0_ld = [0] * 8
    a0_used = [0] * 8
    for tt in range(8):
        buf = xo_buf[tt % 2]
        w = []
        if tt >= 2:
            w.append((s_ve, a0_used[tt - 2]))
        a0_ld[tt] = dma("sync", buf, dXO[tt * 128:(tt + 1) * 128, :], s_ld, w)
        tt_ac = op("scalar", lambda e, buf=buf: nc.scalar.activation(
            sqdump, buf, AF.Square, accum_out=stat_ss[:]),
            s_ac, [(s_ld, a0_ld[tt])] + ([(s_ve, a0_used[tt - 1])] if tt >= 1 else []))
        a0_used[tt] = chain("vector", s_vx, s_ve, [
            lambda e, buf=buf: nc.vector.reduce_sum(stat_s[:], buf, axis=AX.X),
            lambda e, tt=tt: nc.vector.tensor_scalar_mul(
                mo_col[:, tt:tt + 1], stat_s[:], 1.0 / D),
            lambda e, tt=tt: nc.vector.tensor_scalar_mul(
                negmo_col[:, tt:tt + 1], stat_s[:], -1.0 / D),
            lambda e, tt=tt: nc.vector.tensor_mul(
                stat_v[:], mo_col[:, tt:tt + 1], mo_col[:, tt:tt + 1]),
            lambda e: nc.vector.scalar_tensor_tensor(
                stat_v[:], stat_ss[:], 1.0 / D, stat_v[:], ALU.mult, ALU.subtract),
        ], [(s_ld, a0_ld[tt]), (s_ac, tt_ac)])
        chain("scalar", s_ax, s_ac, [
            lambda e: nc.scalar.activation(stat_v[:], stat_v[:], AF.Ln, bias=c_eps[:]),
            lambda e, tt=tt: nc.scalar.activation(
                ro_col[:, tt:tt + 1], stat_v[:], AF.Exp, scale=-0.5),
        ], [(s_ve, a0_used[tt])])
    t_a0_ac = s_ac.n

    wcast_tgt = {}
    for nm, (src, dst) in (("k", (dWK, dWKB)), ("v", (dWV, dWVB))):
        for og in range(NOG):
            wcast_tgt[(nm, og)] = dma(
                "gpsimd", dst[:, og * OG:(og + 1) * OG], src[:, og * OG:(og + 1) * OG],
                s_cast)

    xpan = arena[:, 0:65536].rearrange("p (c t) -> p c t", c=32)
    pan_loaded = [0] * NPAN
    pz_free = [(s_mm, 0), (s_mm, 0)]
    pzv = [None, None]
    wblk_free = [(s_mm, 0), (s_mm, 0)]
    x2_free = [(s_mm, 0), (s_mm, 0)]
    kaff_free = [(s_out, 0), (s_out, 0)]
    ksh_free = [(s_ve, 0), (s_ve, 0)]
    vout_free = [(s_out, 0), (s_out, 0)]
    pc_free = (s_gp, t_consts)
    mrow_free = (s_mm, 0)
    rrow_free = (s_mm, 0)
    zpar = 0
    kpar = 0
    shpar = 0
    vpar = 0
    x2par = 0
    crow_free = (s_mm, 0)
    cvbc_free = (s_ve, 0)

    last_xpan_use = 0

    def emit_bcast(row_ap, dst_ap, dst_f32, row_ready, extra=()):
        nonlocal pc_free
        n = row_ap.shape[-1]
        t_mm = op("tensor", lambda e, r=row_ap, n=n: nc.tensor.matmul(
            pc[:, 0:n], c_ones1[:], r, start=True, stop=True),
            s_mm, [pc_free, row_ready])
        t_cp = op("scalar", lambda e, d=dst_ap, n=n: nc.scalar.copy(d, pc[:, 0:n]),
                  s_ac, [(s_mm, t_mm)] + list(extra))
        pc_free = (s_ac, t_cp)
        return t_cp

    for p in range(NPAN):
        t0 = p * T_PAN
        w = [(s_mm, last_xpan_use)] if p > 0 else [(s_ve, a0_used[7]), (s_ac, t_a0_ac)]
        src = dXT.rearrange("(c p) t -> p c t", p=128)
        for blk in range(4):
            pan_loaded[p] = dma(
                "gpsimd", xpan[:, blk * 8:(blk + 1) * 8, :],
                src[:, blk * 8:(blk + 1) * 8, t0:t0 + T_PAN], s_cast,
                w if blk == 0 else ())
        t_cos = dma("sync", cosbuf[:], dCOSK[:, t0:t0 + T_PAN], s_ld,
                    [kaff_free[0], kaff_free[1]] if p > 0 else [])
        t_sin = dma("sync", sinbuf[:], dSINK[:, t0:t0 + T_PAN], s_ld)

        for tq in range(4):
            sb = tq % 2
            for c in range(32):
                xp = x2par
                x2par ^= 1
                t_sq = op("scalar", lambda e, c=c, tq=tq, xp=xp: nc.scalar.activation(
                    x2buf[xp][:, 0:512], xpan[:, c, tq * 512:(tq + 1) * 512], AF.Square),
                    s_ac, [(s_cast, pan_loaded[p]), x2_free[xp]])
                def statmm(e, c=c, tq=tq, sb=sb, xp=xp):
                    nc.tensor.matmul(pst[sb][0:1, :], c_ones_bf[:],
                                     xpan[:, c, tq * 512:(tq + 1) * 512],
                                     start=(c == 0), stop=(c == 31))
                    return nc.tensor.matmul(pst[2 + sb][0:1, :], c_ones_bf[:],
                                            x2buf[xp][:, 0:512],
                                            start=(c == 0), stop=(c == 31))
                t_smm = op("tensor", statmm, s_mm, [(s_ac, t_sq)])
                x2_free[xp] = (s_mm, t_smm)
            t_fin = chain("vector", s_vx, s_ve, [
                lambda e, sb=sb: nc.vector.tensor_scalar_mul(
                    mrow_s[:], pst[sb][0:1, :], 1.0 / D),
                lambda e: nc.vector.tensor_mul(vtmp_s[:], mrow_s[:], mrow_s[:]),
                lambda e, sb=sb: nc.vector.scalar_tensor_tensor(
                    vtmp_s[:], pst[2 + sb][0:1, :], 1.0 / D, vtmp_s[:],
                    ALU.mult, ALU.subtract),
            ], [(s_mm, s_mm.n), mrow_free, rrow_free])
            t_rr = chain("scalar", s_ax, s_ac, [
                lambda e: nc.scalar.activation(
                    vtmp_s[:], vtmp_s[:], AF.Ln, bias=c_eps[0:1, 0:1]),
                lambda e: nc.scalar.activation(
                    rrow_s[:], vtmp_s[:], AF.Exp, scale=-0.5),
            ], [(s_ve, t_fin)])
            t_c1 = row2col(mrow_s[0:1, :], mcol[:, tq * 4:(tq + 1) * 4],
                           [(s_ve, t_fin)])
            t_c3 = row2col(rrow_s[0:1, :], rcol[:, tq * 4:(tq + 1) * 4],
                           [(s_ac, t_rr)])
            t_bm = emit_bcast(mrow_s[0:1, :], mbc[:, tq * 512:(tq + 1) * 512], False,
                              (s_ve, t_fin))
            mrow_free = (s_sh, t_c1)
            t_br = emit_bcast(rrow_s[0:1, :], rbc[:, tq * 512:(tq + 1) * 512], True,
                              (s_ac, t_rr))
            rrow_free = (s_ac, t_br)

        stats_cols = (s_sh, s_sh.n)
        stats_bc = (s_ac, s_ac.n)
        for og in range(NOG):
            wb = og % 2
            w = [wblk_free[wb]]
            if p == 0:
                w.append((s_cast, wcast_tgt[("k", og)]))
            t_wb = dma("sync", wblk[wb][:],
                       dWKB.rearrange("(c p) o -> p c o", p=128)[:, :, og * OG:(og + 1) * OG],
                       s_wb, w)
            if p == 0:
                def csum(e, wb=wb):
                    for c in range(32):
                        ins = nc.tensor.matmul(pc[0:1, 0:OG], c_ones_bf[:],
                                               wblk[wb][:, c, :],
                                               start=(c == 0), stop=(c == 31))
                    return ins
                t_cs = op("tensor", csum, s_mm, [(s_wb, t_wb), pc_free])
                t_neg = op("vector", lambda e: nc.vector.tensor_scalar_mul(
                    crow[:], pc[0:1, 0:OG], -1.0), s_ve,
                    [(s_mm, t_cs), crow_free])
                pc_free = (s_ve, t_neg)
                t_cc = row2col(crow[0:1, :], negc_k[:, og * 2:og * 2 + 2],
                               [(s_ve, t_neg)])
                crow_free = (s_sh, t_cc)
            for oc in range(2):
                hc = og * 2 + oc
                hp_done = 0
                for tq in range(4):
                    zp = zpar
                    zpar ^= 1
                    def zmm(e, wb=wb, oc=oc, tq=tq, zp=zp):
                        for c in range(32):
                            ins = nc.tensor.matmul(
                                pz[zp][:], wblk[wb][:, c, oc * 128:(oc + 1) * 128],
                                xpan[:, c, tq * 512:(tq + 1) * 512],
                                start=(c == 0), stop=(c == 31))
                        return ins
                    zw = [pz_free[zp], (s_wb, t_wb), (s_cast, pan_loaded[p])]
                    if tq == 0 and oc == 0:
                        zw.append(stats_bc)
                        zw.append(stats_cols)
                        if p == 0:
                            zw.append((s_sh, t_cc))
                    t_z = op("tensor", zmm, s_mm, zw)
                    last_xpan_use = t_z
                    hp = tq // 2
                    xp = x2par
                    x2par ^= 1
                    t1v = x2buf[xp][:, 0:1024].bitcast(F32)
                    def evac(e, tq=tq, hc=hc, zp=zp, xp=xp, hp=hp, kp=kpar):
                        t1v_ = x2buf[xp][:, 0:1024].bitcast(F32)
                        nc.vector.scalar_tensor_tensor(
                            t1v_, mbc[:, tq * 512:(tq + 1) * 512],
                            negc_k[:, hc:hc + 1], pz[zp][:], ALU.mult, ALU.add)
                        return nc.vector.tensor_mul(
                            kaff[kp][:, (tq % 2) * 512:((tq % 2) + 1) * 512],
                            t1v_, rbc[:, tq * 512:(tq + 1) * 512])
                    t_e = op("vector", evac, s_ve,
                             [(s_mm, t_z), x2_free[xp], kaff_free[kpar] if tq % 2 == 0 else (s_mm, 0)])
                    pz_free[zp] = (s_ve, t_e)
                    x2_free[xp] = (s_ve, t_e)
                    if tq % 2 == 1:
                        kp = kpar
                        sp = shpar
                        t_s1 = dma("sync", ksh[sp][0:64, :], kaff[kp][64:128, :],
                                   s_sh, [(s_ve, t_e), ksh_free[sp]])
                        t_s2 = dma("sync", ksh[sp][64:128, :], kaff[kp][0:64, :],
                                   s_sh)
                        t_g = op("gpsimd", lambda e, sp=sp, hp=hp: nc.gpsimd.tensor_mul(
                            ksh[sp][:], ksh[sp][:], sinbuf[:, hp * 1024:(hp + 1) * 1024]),
                            s_gp, [(s_sh, t_s2), (s_ld, t_sin)])
                        def ropefin(e, kp=kp, sp=sp, hp=hp):
                            nc.vector.tensor_mul(kaff[kp][:], kaff[kp][:],
                                                 cosbuf[:, hp * 1024:(hp + 1) * 1024])
                            return nc.vector.tensor_add(kaff[kp][:], kaff[kp][:],
                                                        ksh[sp][:])
                        t_rf = op("vector", ropefin, s_ve,
                                  [(s_sh, t_s2), (s_gp, t_g), (s_ld, t_cos)])
                        ksh_free[sp] = (s_ve, t_rf)
                        t_ko = dma("sync", dKG[hc * 128:(hc + 1) * 128,
                                               t0 + hp * 1024:t0 + (hp + 1) * 1024],
                                   kaff[kp][:], s_out, [(s_ve, t_rf)])
                        kaff_free[kp] = (s_out, t_ko)
                        kpar ^= 1
                        shpar ^= 1
            wblk_free[wb] = (s_mm, s_mm.n)

        for og in range(NOG):
            wb = og % 2
            w = [wblk_free[wb]]
            if p == 0:
                w.append((s_cast, wcast_tgt[("v", og)]))
            t_wb = dma("sync", wblk[wb][:],
                       dWVB.rearrange("(c p) o -> p c o", p=128)[:, :, og * OG:(og + 1) * OG],
                       s_wb, w)
            if p == 0:
                def csumv(e, wb=wb):
                    for c in range(32):
                        ins = nc.tensor.matmul(pc[0:1, 0:OG], c_ones_bf[:],
                                               wblk[wb][:, c, :],
                                               start=(c == 0), stop=(c == 31))
                    return ins
                t_cs = op("tensor", csumv, s_mm, [(s_wb, t_wb), pc_free])
                t_neg = op("vector", lambda e: nc.vector.tensor_scalar_mul(
                    crow[:], pc[0:1, 0:OG], -1.0), s_ve,
                    [(s_mm, t_cs), crow_free])
                pc_free = (s_ve, t_neg)
                t_cc = row2col(crow[0:1, :], negc_v[:, og * 2:og * 2 + 2],
                               [(s_ve, t_neg)])
                crow_free = (s_sh, t_cc)
            t_cv = col2row(negc_v[:, og * 2:og * 2 + 2], cvrow[0:1, :],
                           ([(s_sh, t_cc)] if p == 0 else []) + [pc_free])
            t_cvb = emit_bcast(cvrow[0:1, :], cvbc[:], True, (s_sh, t_cv),
                               extra=[cvbc_free])
            for tt in range(16):
                zp = zpar
                zpar ^= 1
                def vmm(e, wb=wb, tt=tt, zp=zp):
                    for c in range(32):
                        ins = nc.tensor.matmul(
                            pz[zp][:, 0:OG], xpan[:, c, tt * 128:(tt + 1) * 128],
                            wblk[wb][:, c, :], start=(c == 0), stop=(c == 31))
                    return ins
                t_z = op("tensor", vmm, s_mm,
                         [pz_free[zp], (s_wb, t_wb), (s_cast, pan_loaded[p]),
                          (s_ac, t_cvb)])
                last_xpan_use = t_z
                vp = vpar
                vpar ^= 1
                xp = x2par
                x2par ^= 1
                def vevac(e, tt=tt, zp=zp, vp=vp, xp=xp):
                    t1v_ = x2buf[xp][:, 0:512].bitcast(F32)
                    nc.vector.scalar_tensor_tensor(
                        t1v_, cvbc[:], mcol[:, tt:tt + 1], pz[zp][:, 0:OG],
                        ALU.mult, ALU.add)
                    return nc.vector.tensor_scalar_mul(
                        vout[vp][:], t1v_, 1.0) if False else nc.vector.tensor_scalar(
                        vout[vp][:], t1v_, rcol[:, tt:tt + 1], None, ALU.mult)
                t_e = op("vector", vevac, s_ve,
                         [(s_mm, t_z), vout_free[vp], x2_free[xp]])
                pz_free[zp] = (s_ve, t_e)
                x2_free[xp] = (s_ve, t_e)
                t_vo = dma("sync", dVG[t0 + tt * 128:t0 + (tt + 1) * 128,
                                       og * OG:(og + 1) * OG],
                           vout[vp][:], s_out, [(s_ve, t_e)])
                vout_free[vp] = (s_out, t_vo)
            cvbc_free = (s_ve, s_ve.n)
            wblk_free[wb] = (s_mm, s_mm.n)

    sxq = arena[:, 0:32768].rearrange("p (c t) -> p c t", c=32)
    src = dXOT.rearrange("(c p) t -> p c t", p=128)
    t_sxq = 0
    for blk in range(4):
        t_sxq = dma("gpsimd", sxq[:, blk * 8:(blk + 1) * 8, :],
                    src[:, blk * 8:(blk + 1) * 8, :], s_cast,
                    [(s_mm, last_xpan_use)] if blk == 0 else [])
    mo_row = arena[0:1, 32768:34816].bitcast(F32)
    ro_row = arena[0:1, 34816:36864].bitcast(F32)
    t_morow = col2row(mo_col[:, :], mo_row,
                      [(s_ve, a0_used[7]), (s_mm, last_xpan_use)])
    t_rorow = col2row(ro_col[:, :], ro_row, [(s_ac, t_a0_ac)])
    t_cosq = dma("sync", cosbuf[:, 0:1024], dCOSQ, s_ld, [kaff_free[0], kaff_free[1]])
    t_sinq = dma("sync", sinbuf[:, 0:1024], dSINQ, s_ld)
    aq_war = (s_ve, s_ve.n)
    t_bm0 = emit_bcast(mo_row[:, 0:512], mbc[:, 0:512], False, (s_sh, t_morow),
                       extra=[aq_war])
    t_bm1 = emit_bcast(mo_row[:, 512:1024], mbc[:, 512:1024], False, (s_sh, t_morow))
    t_br0 = emit_bcast(ro_row[:, 0:512], rbc[:, 0:512], True, (s_sh, t_rorow),
                       extra=[aq_war])
    t_br1 = emit_bcast(ro_row[:, 512:1024], rbc[:, 512:1024], True, (s_sh, t_rorow))
    t_qbc = (s_ac, s_ac.n)
    for og in range(NOG):
        wb = og % 2
        t_wb = dma("gpsimd", wblk[wb][:],
                   dWQ.rearrange("(c p) o -> p c o", p=128)[:, :, og * OG:(og + 1) * OG],
                   s_cast, [wblk_free[wb]])
        def csq(e, wb=wb):
            for c in range(32):
                ins = nc.tensor.matmul(pc[0:1, 0:OG], c_ones_bf[:], wblk[wb][:, c, :],
                                       start=(c == 0), stop=(c == 31))
            return ins
        t_cs = op("tensor", csq, s_mm, [(s_cast, t_wb), pc_free])
        t_neg = op("vector", lambda e: nc.vector.tensor_scalar_mul(
            crow[:], pc[0:1, 0:OG], -1.0), s_ve, [(s_mm, t_cs), crow_free])
        pc_free = (s_ve, t_neg)
        t_cc = row2col(crow[0:1, :], negc_q[:, og * 2:og * 2 + 2],
                       [(s_ve, t_neg)])
        crow_free = (s_sh, t_cc)
        for oc in range(2):
            hc = og * 2 + oc
            kp = kpar
            for tq in range(2):
                zp = zpar
                zpar ^= 1
                def qmm(e, wb=wb, oc=oc, tq=tq, zp=zp):
                    for c in range(32):
                        ins = nc.tensor.matmul(
                            pz[zp][:], wblk[wb][:, c, oc * 128:(oc + 1) * 128],
                            sxq[:, c, tq * 512:(tq + 1) * 512],
                            start=(c == 0), stop=(c == 31))
                    return ins
                t_z = op("tensor", qmm, s_mm,
                         [pz_free[zp], (s_cast, t_wb), (s_cast, t_sxq), t_qbc,
                          (s_sh, t_cc)])
                xp = x2par
                x2par ^= 1
                def qevac(e, tq=tq, hc=hc, zp=zp, xp=xp, kp=kp):
                    t1v_ = x2buf[xp][:, 0:1024].bitcast(F32)
                    nc.vector.scalar_tensor_tensor(
                        t1v_, mbc[:, tq * 512:(tq + 1) * 512],
                        negc_q[:, hc:hc + 1], pz[zp][:], ALU.mult, ALU.add)
                    return nc.vector.tensor_mul(
                        kaff[kp][:, tq * 512:(tq + 1) * 512], t1v_,
                        rbc[:, tq * 512:(tq + 1) * 512])
                t_e = op("vector", qevac, s_ve,
                         [(s_mm, t_z), x2_free[xp],
                          kaff_free[kp] if tq == 0 else (s_mm, 0)])
                pz_free[zp] = (s_ve, t_e)
                x2_free[xp] = (s_ve, t_e)
            sp = shpar
            t_s1 = dma("sync", ksh[sp][0:64, :], kaff[kp][64:128, :], s_sh,
                       [(s_ve, t_e), ksh_free[sp]])
            t_s2 = dma("sync", ksh[sp][64:128, :], kaff[kp][0:64, :], s_sh)
            t_g = op("gpsimd", lambda e, sp=sp: nc.gpsimd.tensor_mul(
                ksh[sp][:], ksh[sp][:], sinbuf[:, 0:1024]),
                s_gp, [(s_sh, t_s2), (s_ld, t_sinq)])
            def ropefinq(e, kp=kp, sp=sp):
                nc.vector.tensor_mul(kaff[kp][:], kaff[kp][:], cosbuf[:, 0:1024])
                return nc.vector.tensor_add(kaff[kp][:], kaff[kp][:], ksh[sp][:])
            t_rf = op("vector", ropefinq, s_ve,
                      [(s_sh, t_s2), (s_gp, t_g), (s_ld, t_cosq)])
            ksh_free[sp] = (s_ve, t_rf)
            t_ko = dma("sync", dQG[hc * 128:(hc + 1) * 128, :], kaff[kp][:],
                       s_out, [(s_ve, t_rf)])
            kaff_free[kp] = (s_out, t_ko)
            kpar ^= 1
            shpar ^= 1
        wblk_free[wb] = (s_mm, s_mm.n)

    srwt = ar_bf(37376, 256).rearrange("p (c e) -> p c e", c=32)
    sAal = ar_bf(37888, 4096).rearrange("p (c r) -> p c r", c=32)
    sBal = ar_bf(41984, 4096)
    sfgt = [ar_f32(46080, 256), ar_f32(46592, 256)]
    b_t1 = ar_f32(47104, 128)
    b_nca_bc = ar_f32(47360, 128)
    b_midwT = ar_bf(47616, 1024)
    b_ffn_war = (s_ac, s_ac.n)
    t_rwt = dma("gpsimd", srwt, dRWT.rearrange("(c p) e -> p c e", p=128),
                s_cast, [(s_mm, s_mm.n), b_ffn_war])
    t_aal = dma("gpsimd", sAal, dAall.rearrange("(c p) r -> p c r", p=128), s_cast)
    t_bal = dma("gpsimd", sBal, dBall, s_cast)
    def csr(e):
        for c in range(32):
            ins = nc.tensor.matmul(pc[0:1, 0:8], c_ones_bf[:], srwt[:, c, :],
                                   start=(c == 0), stop=(c == 31))
        return ins
    t_cs = op("tensor", csr, s_mm, [(s_cast, t_rwt), pc_free])
    t_neg = op("vector", lambda e: nc.vector.tensor_scalar_mul(
        b_ncr[0:1, 0:8], pc[0:1, 0:8], -1.0), s_ve, [(s_mm, t_cs)])
    pc_free = (s_ve, t_neg)
    t_bcr = emit_bcast(b_ncr[0:1, 0:8], b_ncr_bc[:], True, (s_ve, t_neg))
    def csa(e):
        for c in range(32):
            ins = nc.tensor.matmul(pc[0:1, 0:128], c_ones_bf[:], sAal[:, c, :],
                                   start=(c == 0), stop=(c == 31))
        return ins
    t_cs2 = op("tensor", csa, s_mm, [(s_cast, t_aal), pc_free])
    t_neg2 = op("vector", lambda e: nc.vector.tensor_scalar_mul(
        b_ncr[0:1, :], pc[0:1, 0:128], -1.0), s_ve, [(s_mm, t_cs2), (s_ac, t_bcr)])
    pc_free = (s_ve, t_neg2)
    t_bca = emit_bcast(b_ncr[0:1, :], b_nca_bc, True, (s_ve, t_neg2))

    ptr_free = (s_ve, 0)
    midw_free = (s_mm, 0)
    midwT_used = 0
    for tt in range(8):
        zp = zpar
        zpar ^= 1
        def rmm(e, tt=tt, zp=zp):
            for c in range(32):
                ins = nc.tensor.matmul(pz[zp][:, 0:8], sxq[:, c, tt * 128:(tt + 1) * 128],
                                       srwt[:, c, :], start=(c == 0), stop=(c == 31))
            return ins
        t_z = op("tensor", rmm, s_mm, [pz_free[zp], (s_cast, t_rwt), (s_cast, t_sxq)])
        t_r1 = chain("vector", s_vx, s_ve, [
            lambda e, tt=tt, zp=zp: nc.vector.scalar_tensor_tensor(
                b_lg[:], b_ncr_bc[:], mo_col[:, tt:tt + 1],
                pz[zp][:, 0:8], ALU.mult, ALU.add),
            lambda e, tt=tt: nc.vector.tensor_scalar(
                b_lg[:], b_lg[:], ro_col[:, tt:tt + 1], None, ALU.mult),
            lambda e: nc.vector.reduce_max(b_m1[:], b_lg[:], axis=AX.X),
            lambda e: nc.vector.tensor_scalar_mul(b_negm1[:], b_m1[:], -1.0),
            lambda e: nc.vector.tensor_single_scalar(
                b_eq[:], b_lg[:], b_m1[:], ALU.is_equal),
            lambda e: nc.vector.scalar_tensor_tensor(
                b_msk[:], b_eq[:], -1e30, b_lg[:], ALU.mult, ALU.add),
            lambda e: nc.vector.reduce_max(b_m2[:], b_msk[:], axis=AX.X),
        ], [(s_mm, t_z), (s_ac, t_bca)])
        pz_free[zp] = (s_ve, t_r1)
        t_s1 = op("scalar", lambda e: nc.scalar.activation(
            b_s1[:], b_lg[:], AF.Exp, bias=b_negm1[:]), s_ac, [(s_ve, t_r1)])
        t_r2 = chain("vector", s_vx, s_ve, [
            lambda e: nc.vector.tensor_single_scalar(
                b_eq[:], b_lg[:], b_m2[:], ALU.is_ge),
            lambda e: nc.vector.tensor_mul(b_s1[:], b_s1[:], b_eq[:]),
            lambda e: nc.vector.reduce_sum(b_den[:], b_s1[:], axis=AX.X),
            lambda e: nc.vector.reciprocal(b_winv[:], b_den[:]),
            lambda e: nc.vector.tensor_scalar(
                b_s1[:], b_s1[:], b_winv[:], None, ALU.mult),
            lambda e, tt=tt: nc.vector.tensor_scalar(
                b_wro[:, tt, :], b_s1[:], ro_col[:, tt:tt + 1], None, ALU.mult),
        ], [(s_ac, t_s1)])
        zp = zpar
        zpar ^= 1
        def amm(e, tt=tt, zp=zp):
            for c in range(32):
                ins = nc.tensor.matmul(pz[zp][:, 0:128], sxq[:, c, tt * 128:(tt + 1) * 128],
                                       sAal[:, c, :], start=(c == 0), stop=(c == 31))
            return ins
        t_za = op("tensor", amm, s_mm, [pz_free[zp], (s_cast, t_aal)])
        def midw2(e, tt=tt):
            win = b_wro[:, tt, :].unsqueeze(2).broadcast_to([128, 8, 16])
            return nc.vector.tensor_tensor(
                b_midw[:].rearrange("p (e r) -> p e r", e=8),
                b_t1.rearrange("p (e r) -> p e r", e=8), win, ALU.mult)
        t_mw = chain("vector", s_vx, s_ve, [
            lambda e, tt=tt, zp=zp: nc.vector.scalar_tensor_tensor(
                b_t1, b_nca_bc, mo_col[:, tt:tt + 1],
                pz[zp][:, 0:128], ALU.mult, ALU.add),
            midw2,
        ], [(s_mm, t_za), (s_ve, t_r2), midw_free, ptr_free])
        pz_free[zp] = (s_ve, t_mw)
        t_tr = op("tensor", lambda e: nc.tensor.transpose(ptr[:], b_midw[:], c_id[:]),
                  s_mm, [(s_ve, t_mw), (s_ld, t_id), ptr_free])
        midw_free = (s_mm, t_tr)
        t_cpt = op("vector", lambda e, tt=tt: nc.vector.tensor_copy(
            b_midwT[:, tt * 128:(tt + 1) * 128], ptr[:]), s_ve, [(s_mm, t_tr)])
        ptr_free = (s_ve, t_cpt)
        for od in range(8):
            zp = zpar
            zpar ^= 1
            t_zi = op("tensor", lambda e, tt=tt, od=od, zp=zp: nc.tensor.matmul(
                pz[zp][:], b_midwT[:, tt * 128:(tt + 1) * 128],
                sBal[:, od * 512:(od + 1) * 512], start=True, stop=True),
                s_mm, [pz_free[zp], (s_ve, t_cpt), (s_cast, t_bal)])
            xp = x2par
            x2par ^= 1
            def ievac(e, zp=zp, xp=xp):
                t1v_ = x2buf[xp][:, 0:1024].bitcast(F32)
                return nc.vector.tensor_scalar_mul(t1v_, pz[zp][:], 2.0)
            t_ie = op("vector", ievac, s_ve, [(s_mm, t_zi), x2_free[xp]])
            pz_free[zp] = (s_ve, t_ie)
            t_fo = dma("sync", dFG[tt * 128:(tt + 1) * 128, od * 512:(od + 1) * 512],
                       x2buf[xp][:, 0:1024].bitcast(F32), s_out, [(s_ve, t_ie)])
            x2_free[xp] = (s_out, t_fo)
    midwT_used = s_mm.n
    fg_written = s_out.n
    fgt_free = [(s_ve, 0), (s_ve, 0)]
    fpar = 0
    for og in range(NOG):
        wb = og % 2
        t_wb = dma("gpsimd", wblk[wb][:],
                   dBWT.rearrange("(c p) o -> p c o", p=128)[:, :, og * OG:(og + 1) * OG],
                   s_cast, [wblk_free[wb]])
        def csb(e, wb=wb):
            for c in range(32):
                ins = nc.tensor.matmul(pc[0:1, 0:OG], c_ones_bf[:], wblk[wb][:, c, :],
                                       start=(c == 0), stop=(c == 31))
            return ins
        t_cs = op("tensor", csb, s_mm, [(s_cast, t_wb), pc_free])
        t_neg = op("vector", lambda e: nc.vector.tensor_scalar_mul(
            crow[:], pc[0:1, 0:OG], -1.0), s_ve, [(s_mm, t_cs), crow_free])
        pc_free = (s_ve, t_neg)
        t_cvb = emit_bcast(crow[0:1, :], cvbc[:], True, (s_ve, t_neg),
                           extra=[cvbc_free])
        crow_free = pc_free
        for tt in range(8):
            zp = zpar
            zpar ^= 1
            def bmm(e, wb=wb, tt=tt, zp=zp):
                for c in range(32):
                    ins = nc.tensor.matmul(pz[zp][:, 0:OG],
                                           sxq[:, c, tt * 128:(tt + 1) * 128],
                                           wblk[wb][:, c, :], start=(c == 0), stop=(c == 31))
                return ins
            t_z = op("tensor", bmm, s_mm, [pz_free[zp], (s_cast, t_wb), (s_ac, t_cvb)])
            wblk_free[wb] = (s_mm, t_z)
            fp = fpar
            fpar ^= 1
            t_fl = dma("sync", sfgt[fp], dFG[tt * 128:(tt + 1) * 128, og * OG:(og + 1) * OG],
                       s_ld, [fgt_free[fp], (s_out, fg_written)])
            xp = x2par
            x2par ^= 1
            def bevac(e, tt=tt, zp=zp, fp=fp, xp=xp):
                t1v_ = x2buf[xp][:, 0:512].bitcast(F32)
                nc.vector.scalar_tensor_tensor(t1v_, cvbc[:], mo_col[:, tt:tt + 1],
                                               pz[zp][:, 0:OG], ALU.mult, ALU.add)
                return nc.vector.scalar_tensor_tensor(
                    sfgt[fp], t1v_, ro_col[:, tt:tt + 1], sfgt[fp], ALU.mult, ALU.add)
            t_be = op("vector", bevac, s_ve, [(s_mm, t_z), (s_ld, t_fl), x2_free[xp]])
            pz_free[zp] = (s_ve, t_be)
            x2_free[xp] = (s_ve, t_be)
            t_fo = dma("sync", dFG[tt * 128:(tt + 1) * 128, og * OG:(og + 1) * OG],
                       sfgt[fp], s_out, [(s_ve, t_be)])
            fgt_free[fp] = (s_out, t_fo)
        cvbc_free = (s_ve, s_ve.n)
    b_last_mm = s_mm.n
    b_last_out = s_out.n

    mA = ar_bf(0, 16384).rearrange("p (n q) -> p n q", n=NTA)
    mB = ar_bf(16384, 32768).rearrange("p (n q) -> p n q", n=NTB)
    Kh = [ar_bf(49152, 8192), ar_bf(57344, 8192)]
    Vh = [wblk[0][:].rearrange("p a b -> p (a b)").rearrange("p (n d) -> p n d", d=128),
          wblk[1][:].rearrange("p a b -> p (a b)").rearrange("p (n d) -> p n d", d=128)]
    qh = [kaff[0][:], kaff[1][:]]
    PT = [mbc[:, i * 512:(i + 1) * 512] for i in range(4)]
    linv = x2buf[0][:, 0:1024].bitcast(F32)
    lbc = [rbc[:, 0:512], rbc[:, 512:1024]]
    aout = [ksh[0][:, 0:512], ksh[0][:, 512:1024]]

    t_mA = dma("sync", mA, dMA, s_ld, [(s_mm, b_last_mm), (s_out, b_last_out)])
    t_mB = dma("sync", mB, dMB, s_ld)
    vg_r = dVG.rearrange("(n p) d -> p n d", p=128)

    c_start_ve = (s_ve, s_ve.n)
    kh_free = [c_start_ve, c_start_ve]
    vh_free = [(s_mm, b_last_mm), (s_mm, b_last_mm)]
    qh_free = [kaff_free[0], kaff_free[1]]
    pt_free = [c_start_ve] * 4
    ps_free = [pz_free[0], pz_free[1]]
    po_free = [c_start_ve, c_start_ve]
    pl_free = [c_start_ve, c_start_ve]
    lbc_free = [c_start_ve, c_start_ve]
    aout_free = [ksh_free[0], ksh_free[0]]

    kload = [0, 0]
    vload = [0, 0]
    qload = [0, 0]
    sctr = 0
    ptctr = 0
    ints_written = (s_out, s_out.n)
    for h in range(H):
        par = h % 2
        kload[par] = dma("sync", Kh[par], dKG[h * 128:(h + 1) * 128, :], s_ld,
                         [kh_free[par]] + ([ints_written] if h == 0 else []))
        qload[par] = dma("sync", qh[par], dQG[h * 128:(h + 1) * 128, :], s_ld,
                         [qh_free[par]])
        for j4 in range(4):
            vload[par] = dma("scalar", Vh[par][:, j4 * 16:(j4 + 1) * 16, :],
                             vg_r[:, j4 * 16:(j4 + 1) * 16, h * 128:(h + 1) * 128],
                             s_sh, ([vh_free[par]] + ([ints_written] if h == 0 else []))
                             if j4 == 0 else [])
        for (cidx, ntiles, qoff, po, pl, msk) in (
                (0, NTA, 0, 0, 2, mA), (1, NTB, 512, 1, 3, mB)):
            for j in range(ntiles):
                sp = sctr % 2
                sctr += 1
                ptp = ptctr % 4
                ptctr += 1
                t_smm = op("tensor", lambda e, par=par, j=j, qoff=qoff, sp=sp:
                           nc.tensor.matmul(pz[sp][:], Kh[par][:, j * 128:(j + 1) * 128],
                                            qh[par][:, qoff:qoff + 512],
                                            start=True, stop=True),
                           s_mm, [ps_free[sp], (s_ld, kload[par]), (s_ld, qload[par])])
                t_exp = op("scalar", lambda e, sp=sp, ptp=ptp: nc.scalar.activation(
                    PT[ptp], pz[sp][:], AF.Exp, bias=c_negC[:], scale=SM_SCALE),
                    s_ac, [(s_mm, t_smm), pt_free[ptp], (s_gp, t_consts)])
                ps_free[sp] = (s_ac, t_exp)
                t_msk = op("gpsimd", lambda e, ptp=ptp, msk=msk, j=j:
                           nc.gpsimd.tensor_mul(PT[ptp], PT[ptp], msk[:, j, :]),
                           s_gp, [(s_ac, t_exp), (s_ld, t_mB)])
                def lpv(e, par=par, j=j, po=po, pl=pl, ptp=ptp,
                        st=(j == 0), sp_=(j == ntiles - 1)):
                    nc.tensor.matmul(pst[2 + po][0:1, :], c_ones_bf[:], PT[ptp],
                                     start=st, stop=sp_)
                    return nc.tensor.matmul(pst[po][:], Vh[par][:, j, :], PT[ptp],
                                            start=st, stop=sp_)
                w = [(s_gp, t_msk), (s_sh, vload[par])]
                if j == 0:
                    w += [po_free[po], pl_free[po]]
                t_lpv = op("tensor", lpv, s_mm, w)
                pt_free[ptp] = (s_mm, t_lpv)
            t_rec = op("vector", lambda e, pl=pl: nc.vector.reciprocal(
                linv[0:1, :], pst[pl][0:1, :]), s_ve, [(s_mm, t_lpv)])
            t_bc = op("tensor", lambda e: nc.tensor.matmul(
                pc[:], c_ones1[:], linv[0:1, :], start=True, stop=True),
                s_mm, [(s_ve, t_rec), pc_free])
            t_cp = op("scalar", lambda e, cidx=cidx: nc.scalar.copy(
                lbc[cidx], pc[:]), s_ac, [(s_mm, t_bc), lbc_free[cidx]])
            pc_free = (s_ac, t_cp)
            def fin(e, po=po, cidx=cidx):
                return nc.vector.tensor_mul(aout[cidx], pst[po][:], lbc[cidx])
            t_fin = op("vector", fin, s_ve,
                       [(s_ac, t_cp), aout_free[cidx]])
            po_free[po] = (s_ve, t_fin)
            pl_free[po] = (s_ve, t_rec)
            lbc_free[cidx] = (s_ve, t_fin)
            t_ao = dma("sync", dAG[h * 128:(h + 1) * 128, qoff:qoff + 512],
                       aout[cidx], s_out, [(s_ve, t_fin)])
            aout_free[cidx] = (s_out, t_ao)
        kh_free[par] = (s_mm, s_mm.n)
        vh_free[par] = (s_mm, s_mm.n)
        qh_free[par] = (s_mm, s_mm.n)
    c_last_mm = s_mm.n
    c_last_gp = s_gp.n
    c_last_out = s_out.n

    sag = arena[:, 0:32768].rearrange("p (c t) -> p c t", c=32)
    woblk = [arena[:, 32768:49152].rearrange("p (c o) -> p c o", c=32),
             arena[:, 49152:65536].rearrange("p (c o) -> p c o", c=32)]
    sxot = [x2buf[0][:, 0:1024].bitcast(F32), x2buf[1][:, 0:1024].bitcast(F32)]
    sfgt2 = [rbc[:, 0:512], rbc[:, 512:1024]]
    syb = [rbc[:, 1024:1536], rbc[:, 1536:2048]]

    t_ag = dma("sync", sag, dAG.rearrange("(c p) t -> p c t", p=128), s_ld,
               [(s_gp, c_last_gp), (s_mm, c_last_mm), (s_out, c_last_out)])
    wo_free = [(s_mm, c_last_mm), (s_mm, c_last_mm)]
    xot_free = [(s_mm, c_last_mm), (s_mm, c_last_mm)]
    fgt2_free = [(s_ve, s_ve.n), (s_ve, s_ve.n)]
    yb_free = [(s_st, 0), (s_st, 0)]
    dpar = 0
    for og in range(8):
        wb = og % 2
        t_wb = dma("gpsimd", woblk[wb],
                   dWO.rearrange("(c p) o -> p c o", p=128)[:, :, og * 512:(og + 1) * 512],
                   s_cast, [wo_free[wb]])
        for tt in range(8):
            zp = zpar
            zpar ^= 1
            def dmm(e, wb=wb, tt=tt, zp=zp):
                for c in range(32):
                    ins = nc.tensor.matmul(pz[zp][:], sag[:, c, tt * 128:(tt + 1) * 128],
                                           woblk[wb][:, c, :], start=(c == 0), stop=(c == 31))
                return ins
            t_z = op("tensor", dmm, s_mm, [pz_free[zp], (s_cast, t_wb), (s_ld, t_ag)])
            wo_free[wb] = (s_mm, t_z)
            dp = dpar
            dpar ^= 1
            t_xl = dma("sync", sxot[dp][:, 0:512],
                       dXO[tt * 128:(tt + 1) * 128, og * 512:(og + 1) * 512], s_ld,
                       [xot_free[dp]])
            t_fl = dma("sync", sfgt2[dp],
                       dFG[tt * 128:(tt + 1) * 128, og * 512:(og + 1) * 512], s_ld,
                       [fgt2_free[dp]])
            def devac(e, zp=zp, dp=dp):
                nc.vector.tensor_add(syb[dp], pz[zp][:], sfgt2[dp])
                return nc.vector.tensor_add(syb[dp], syb[dp], sxot[dp][:, 0:512])
            t_de = op("vector", devac, s_ve,
                      [(s_mm, t_z), (s_ld, t_xl), (s_ld, t_fl), yb_free[dp]])
            pz_free[zp] = (s_ve, t_de)
            xot_free[dp] = (s_ve, t_de)
            fgt2_free[dp] = (s_ve, t_de)
            t_y = dma("sync", dY[tt * 128:(tt + 1) * 128, og * 512:(og + 1) * 512],
                      syb[dp], s_st, [(s_ve, t_de)])
            yb_free[dp] = (s_st, t_y)

    def final_wait(e):
        e.wait_ge(s_st.h, s_st.n)
    emit("sync", final_wait)

    with nc.Block() as block:
        @block.tensor
        def _(eng):
            for fn in Q["tensor"]:
                fn(eng)

        @block.vector
        def _(eng):
            for fn in Q["vector"]:
                fn(eng)

        @block.scalar
        def _(eng):
            for fn in Q["scalar"]:
                fn(eng)

        @block.gpsimd
        def _(eng):
            for fn in Q["gpsimd"]:
                fn(eng)

        @block.sync
        def _(eng):
            for fn in Q["sync"]:
                fn(eng)

    return nc


def _rope_tables(pos):
    d = np.arange(128)
    inv = 10000.0 ** (-(2.0 * (d % 64)) / 128.0)
    ang = inv[:, None] * pos[None, :].astype(np.float64)
    cos = np.cos(ang).astype(np.float32)
    sin = np.sin(ang).astype(np.float32)
    sin[0:64] *= -1.0
    return cos.astype(ml_dtypes.bfloat16), sin.astype(ml_dtypes.bfloat16)


def _masks(core):
    cA, cB = core, 15 - core
    ql = np.arange(512)
    p = np.arange(128)
    mA = np.zeros((128, NTA, 512), dtype=np.float32)
    mB = np.zeros((128, NTB, 512), dtype=np.float32)
    for j in range(NTA):
        kg = 128 * j + p
        mA[:, j, :] = (kg[:, None] <= (512 * cA + ql)[None, :])
    for j in range(NTB):
        kg = 128 * j + p
        mB[:, j, :] = (kg[:, None] <= (512 * cB + ql)[None, :])
    return mA.astype(ml_dtypes.bfloat16), mB.astype(ml_dtypes.bfloat16)


_NC_CACHE = {}


def kernel(L0_attn_hidden_states, L0_attn_wq, L0_attn_wk, L0_attn_wv, L0_attn_wo,
           L0_attn_kq1, L0_attn_kq2, L0_attn_kk1, L0_attn_kk2, L0_attn_scalingD,
           L0_attn_interaction_k, L0_attn_scalingD2, L0_ffn_base_W, L0_ffn_router_W,
           L0_ffn_lora_A, L0_ffn_lora_B):
    x = np.ascontiguousarray(np.asarray(L0_attn_hidden_states, dtype=np.float32)[0])
    xT = np.ascontiguousarray(x.T)
    wq = np.ascontiguousarray(L0_attn_wq, dtype=np.float32)
    wk = np.ascontiguousarray(L0_attn_wk, dtype=np.float32)
    wv = np.ascontiguousarray(L0_attn_wv, dtype=np.float32)
    wo = np.ascontiguousarray(L0_attn_wo, dtype=np.float32)
    bWT = np.ascontiguousarray(np.asarray(L0_ffn_base_W, dtype=np.float32).T)
    rWT = np.ascontiguousarray(np.asarray(L0_ffn_router_W, dtype=np.float32).T)
    Aall = np.ascontiguousarray(
        np.asarray(L0_ffn_lora_A, dtype=np.float32).transpose(1, 0, 2).reshape(D, 128))
    Ball = np.ascontiguousarray(
        np.asarray(L0_ffn_lora_B, dtype=np.float32).reshape(128, D))
    ident = np.eye(128, dtype=ml_dtypes.bfloat16)
    cosk, sink = _rope_tables(np.arange(S))

    if "nc" not in _NC_CACHE:
        _NC_CACHE["nc"] = build()
    nc = _NC_CACHE["nc"]

    in_maps = []
    for i in range(NCORE):
        cA, cB = i, 15 - i
        rows = np.r_[cA * 512:(cA + 1) * 512, cB * 512:(cB + 1) * 512]
        xo = np.ascontiguousarray(x[rows])
        xoT = np.ascontiguousarray(xT[:, rows])
        cosq, sinq = _rope_tables(rows)
        mA, mB = _masks(i)
        in_maps.append({
            "xT": xT, "xo": xo, "xoT": xoT,
            "wq": wq, "wk": wk, "wv": wv, "wo": wo,
            "bWT": bWT, "rWT": rWT, "Aall": Aall, "Ball": Ball,
            "cosq": cosq, "sinq": sinq, "cosk": cosk, "sink": sink,
            "maskA": mA, "maskB": mB, "ident": ident,
        })
    res = run_bass_kernel_spmd(nc, in_maps, core_ids=list(range(NCORE)))
    y = np.empty((S, D), dtype=np.float32)
    for i in range(NCORE):
        yi = np.asarray(res.results[i]["y"])
        y[i * 512:(i + 1) * 512] = yi[0:512]
        y[(15 - i) * 512:(16 - i) * 512] = yi[512:1024]
    return y.reshape(1, S, D)


# version 03
# speedup vs baseline: 1.4589x; 1.4589x over previous
import math
import numpy as np
import ml_dtypes
import concourse.bass as bass
import concourse.mybir as mybir
from concourse.bass_utils import run_bass_kernel_spmd

F32 = mybir.dt.float32
BF16 = mybir.dt.bfloat16
AF = mybir.ActivationFunctionType
ALU = mybir.AluOpType
AX = mybir.AxisListType

S = 8192
D = 4096
H = 32
NCORE = 8
T_OWN = 1024
T_PAN = 2048
NPAN = S // T_PAN
OG = 256
NOG = D // OG
EXP_C = 12.0
SM_SCALE = 1.0 / math.sqrt(128.0)
NTA = 32
NTB = 64
EPS = 1e-5


def build(debug=False):
    nc = bass.Bass("TRN2", target_bir_lowering=False)

    dXT = nc.dram_tensor("xT", [D, S], F32, kind="ExternalInput").ap()
    dXO = nc.dram_tensor("xo", [T_OWN, D], F32, kind="ExternalInput").ap()
    dXOT = nc.dram_tensor("xoT", [D, T_OWN], F32, kind="ExternalInput").ap()
    dWQ = nc.dram_tensor("wq", [D, D], F32, kind="ExternalInput").ap()
    dWK = nc.dram_tensor("wk", [D, D], F32, kind="ExternalInput").ap()
    dWV = nc.dram_tensor("wv", [D, D], F32, kind="ExternalInput").ap()
    dWO = nc.dram_tensor("wo", [D, D], F32, kind="ExternalInput").ap()
    dBWT = nc.dram_tensor("bWT", [D, D], F32, kind="ExternalInput").ap()
    dRWT = nc.dram_tensor("rWT", [D, 8], F32, kind="ExternalInput").ap()
    dAall = nc.dram_tensor("Aall", [D, 128], F32, kind="ExternalInput").ap()
    dBall = nc.dram_tensor("Ball", [128, D], F32, kind="ExternalInput").ap()
    dCOSQ = nc.dram_tensor("cosq", [128, T_OWN], BF16, kind="ExternalInput").ap()
    dSINQ = nc.dram_tensor("sinq", [128, T_OWN], BF16, kind="ExternalInput").ap()
    dCOSK = nc.dram_tensor("cosk", [128, S], BF16, kind="ExternalInput").ap()
    dSINK = nc.dram_tensor("sink", [128, S], BF16, kind="ExternalInput").ap()
    dMA = nc.dram_tensor("maskA", [128, NTA, 512], BF16, kind="ExternalInput").ap()
    dMB = nc.dram_tensor("maskB", [128, NTB, 512], BF16, kind="ExternalInput").ap()
    dID = nc.dram_tensor("ident", [128, 128], BF16, kind="ExternalInput").ap()

    dY = nc.dram_tensor("y", [T_OWN, D], F32, kind="ExternalOutput").ap()
    ikind = "ExternalOutput" if debug else "Internal"
    dWKB = nc.dram_tensor("wkb", [D, D], BF16, kind="Internal").ap()
    dWVB = nc.dram_tensor("wvb", [D, D], BF16, kind="Internal").ap()
    dKG = nc.dram_tensor("Kg", [D, S], BF16, kind=ikind).ap()
    dVG = nc.dram_tensor("Vg", [S, D], BF16, kind=ikind).ap()
    dQG = nc.dram_tensor("qg", [D, T_OWN], BF16, kind=ikind).ap()
    dAG = nc.dram_tensor("ag", [D, T_OWN], BF16, kind=ikind).ap()
    dFG = nc.dram_tensor("fg", [T_OWN, D], F32, kind=ikind).ap()
    dSCR = nc.dram_tensor("scr", [256, 1024], F32, kind="Internal").ap()

    A_ = nc.alloc_sbuf_tensor
    c_id = A_("c_id", [128, 128], BF16)
    c_ones_bf = A_("c_ones_bf", [128, 1], BF16)
    c_ones1 = A_("c_ones1", [1, 128], F32)
    c_negC = A_("c_negC", [128, 1], F32)
    c_eps = A_("c_eps", [128, 1], F32)

    arena = A_("arena", [128, 65536], BF16)
    wblk = [A_(f"wblk{i}", [128, 32, OG], BF16) for i in range(2)]
    x2buf = [A_(f"x2buf{i}", [128, 1024], BF16) for i in range(2)]
    mrow_s = A_("mrow_s", [1, 512], F32)
    rrow_s = A_("rrow_s", [1, 512], F32)
    vtmp_s = A_("vtmp_s", [1, 512], F32)
    mcol = A_("mcol", [128, 16], F32)
    rcol = A_("rcol", [128, 16], F32)
    mbc = A_("mbc", [128, T_PAN], BF16)
    rbc = A_("rbc", [128, T_PAN], F32)
    negc_k = A_("negc_k", [128, 32], F32)
    negc_v = A_("negc_v", [128, 32], F32)
    negc_q = A_("negc_q", [128, 32], F32)
    crow = A_("crow", [1, OG], F32)
    cvrow = A_("cvrow", [1, OG], F32)
    cvbc = A_("cvbc", [128, OG], F32)
    kaff = [A_(f"kaff{i}", [128, 1024], BF16) for i in range(2)]
    ksh = [A_(f"ksh{i}", [128, 1024], BF16) for i in range(2)]
    vout = [A_(f"vout{i}", [128, OG], BF16) for i in range(2)]
    cosbuf = A_("cosbuf", [128, 2048], BF16)
    sinbuf = A_("sinbuf", [128, 2048], BF16)
    mo_col = A_("mo_col", [128, 8], F32)
    negmo_col = A_("negmo_col", [128, 8], F32)
    ro_col = A_("ro_col", [128, 8], F32)
    stat_s = A_("stat_s", [128, 1], F32)
    stat_ss = A_("stat_ss", [128, 1], F32)
    stat_v = A_("stat_v", [128, 1], F32)
    b_lg = A_("b_lg", [128, 8], F32)
    b_eq = A_("b_eq", [128, 8], F32)
    b_msk = A_("b_msk", [128, 8], F32)
    b_m1 = A_("b_m1", [128, 1], F32)
    b_negm1 = A_("b_negm1", [128, 1], F32)
    b_m2 = A_("b_m2", [128, 1], F32)
    b_s1 = A_("b_s1", [128, 8], F32)
    b_den = A_("b_den", [128, 1], F32)
    b_winv = A_("b_winv", [128, 1], F32)
    b_wro = A_("b_wro", [128, 8, 8], F32)
    b_midw = A_("b_midw", [128, 128], BF16)
    b_ncr = A_("b_ncr", [1, 128], F32)
    b_ncr_bc = A_("b_ncr_bc", [128, 8], F32)

    P_ = nc.alloc_psum_tensor
    pz = [P_(f"pz{i}", [128, 512], F32) for i in range(2)]
    pst = [P_(f"pst{i}", [128, 512], F32) for i in range(4)]
    pc = P_("pc", [128, 512], F32)
    ptr = P_("ptr", [128, 128], BF16)

    ENGS = ("tensor", "vector", "scalar", "gpsimd", "sync")
    Q = {e: [] for e in ENGS}

    class Sem:
        def __init__(self, name):
            self.h = nc.alloc_semaphore(name)
            self.n = 0
    s_ld = Sem("s_ld")
    s_cast = Sem("s_cast")
    s_wb = Sem("s_wb")
    s_sh = Sem("s_sh")
    s_out = Sem("s_out")
    s_mm = Sem("s_mm")
    s_ve = Sem("s_ve")
    s_ac = Sem("s_ac")
    s_gp = Sem("s_gp")
    s_st = Sem("s_st")
    s_vx = Sem("s_vx")
    s_ax = Sem("s_ax")

    def emit(eng, fn):
        Q[eng].append(fn)

    def dma(eng_q, out, in_, sem, waits=(), transpose=False, slow=False):
        sem.n += 16
        tgt = sem.n
        def fn(e, out=out, in_=in_, waits=tuple(waits)):
            for (s, t) in waits:
                e.wait_ge(s.h, t)
            e.dma_start(out=out, in_=in_, transpose=transpose,
                        allow_slow_non_contiguous=slow).then_inc(sem.h, 16)
        emit(eng_q, fn)
        return tgt

    def chain(eng, chain_sem, final_sem, steps, waits):
        for i, fn_ in enumerate(steps[:-1]):
            op(eng, fn_, chain_sem,
               list(waits) if i == 0 else [(chain_sem, chain_sem.n)])
        return op(eng, steps[-1], final_sem,
                  [(chain_sem, chain_sem.n)] if len(steps) > 1 else list(waits))

    scr_slot = [0]

    def row2col(row_ap, col_ap, waits, pp=128):
        sl = scr_slot[0]; scr_slot[0] += 1
        n = row_ap.shape[-1]
        t1 = dma("sync", dSCR[sl, 0:n], row_ap, s_sh, waits)
        t2 = dma("sync", col_ap,
                 dSCR[sl, 0:n].rearrange("(t p) -> p t", p=pp), s_sh, [(s_sh, t1)],
                 slow=True)
        return t2

    def col2row(col_ap, row_ap, waits, pp=128):
        sl = scr_slot[0]; scr_slot[0] += 1
        n = row_ap.shape[-1]
        t1 = dma("sync", dSCR[sl, 0:n].rearrange("(t p) -> p t", p=pp), col_ap,
                 s_sh, waits, slow=True)
        t2 = dma("sync", row_ap, dSCR[sl, 0:n], s_sh, [(s_sh, t1)])
        return t2

    def op(eng, build_fn, sem=None, waits=(), k=1):
        tgt = None
        if sem is not None:
            sem.n += k
            tgt = sem.n
        def fn(e, waits=tuple(waits)):
            for (s, t) in waits:
                e.wait_ge(s.h, t)
            ins = build_fn(e)
            if sem is not None:
                ins.then_inc(sem.h, k)
        emit(eng, fn)
        return tgt

    def ar_bf(lo, n):
        return arena[:, lo:lo + n]
    def ar_f32(lo, n):
        return arena[:, lo:lo + 2 * n].bitcast(F32)

    op("gpsimd", lambda e: (nc.gpsimd.memset(c_ones_bf[:], 1.0),
                            nc.gpsimd.memset(c_ones1[:], 1.0),
                            nc.gpsimd.memset(c_negC[:], -EXP_C),
                            nc.gpsimd.memset(c_eps[:], EPS))[-1], s_gp)
    t_consts = s_gp.n
    t_id = dma("sync", c_id[:], dID, s_ld)

    xo_buf = [ar_f32(0, 4096), ar_f32(8192, 4096)]
    sqdump = ar_f32(16384, 4096)
    a0_ld = [0] * 8
    a0_used = [0] * 8
    for tt in range(8):
        buf = xo_buf[tt % 2]
        w = []
        if tt >= 2:
            w.append((s_ve, a0_used[tt - 2]))
        a0_ld[tt] = dma("sync", buf, dXO[tt * 128:(tt + 1) * 128, :], s_ld, w)
        tt_ac = op("scalar", lambda e, buf=buf: nc.scalar.activation(
            sqdump, buf, AF.Square, accum_out=stat_ss[:]),
            s_ac, [(s_ld, a0_ld[tt])] + ([(s_ve, a0_used[tt - 1])] if tt >= 1 else []))
        a0_used[tt] = chain("vector", s_vx, s_ve, [
            lambda e, buf=buf: nc.vector.reduce_sum(stat_s[:], buf, axis=AX.X),
            lambda e, tt=tt: nc.vector.tensor_scalar_mul(
                mo_col[:, tt:tt + 1], stat_s[:], 1.0 / D),
            lambda e, tt=tt: nc.vector.tensor_scalar_mul(
                negmo_col[:, tt:tt + 1], stat_s[:], -1.0 / D),
            lambda e, tt=tt: nc.vector.tensor_mul(
                stat_v[:], mo_col[:, tt:tt + 1], mo_col[:, tt:tt + 1]),
            lambda e: nc.vector.scalar_tensor_tensor(
                stat_v[:], stat_ss[:], 1.0 / D, stat_v[:], ALU.mult, ALU.subtract),
        ], [(s_ld, a0_ld[tt]), (s_ac, tt_ac)])
        chain("scalar", s_ax, s_ac, [
            lambda e: nc.scalar.activation(stat_v[:], stat_v[:], AF.Ln, bias=c_eps[:]),
            lambda e, tt=tt: nc.scalar.activation(
                ro_col[:, tt:tt + 1], stat_v[:], AF.Exp, scale=-0.5),
        ], [(s_ve, a0_used[tt])])
    t_a0_ac = s_ac.n

    wcast_tgt = {}
    for nm, (src, dst) in (("k", (dWK, dWKB)), ("v", (dWV, dWVB))):
        for og in range(NOG):
            wcast_tgt[(nm, og)] = dma(
                "gpsimd", dst[:, og * OG:(og + 1) * OG], src[:, og * OG:(og + 1) * OG],
                s_cast)

    xpan = arena[:, 0:65536].rearrange("p (c t) -> p c t", c=32)
    pan_loaded = [0] * NPAN
    pz_free = [(s_mm, 0), (s_mm, 0)]
    pzv = [None, None]
    wblk_free = [(s_mm, 0), (s_mm, 0)]
    x2_free = [(s_mm, 0), (s_mm, 0)]
    kaff_free = [(s_out, 0), (s_out, 0)]
    ksh_free = [(s_ve, 0), (s_ve, 0)]
    vout_free = [(s_out, 0), (s_out, 0)]
    pc_free = (s_gp, t_consts)
    mrow_free = (s_mm, 0)
    rrow_free = (s_mm, 0)
    zpar = 0
    kpar = 0
    shpar = 0
    vpar = 0
    x2par = 0
    crow_free = (s_mm, 0)
    cvbc_free = (s_ve, 0)

    last_xpan_use = 0

    def emit_bcast(row_ap, dst_ap, dst_f32, row_ready, extra=()):
        nonlocal pc_free
        n = row_ap.shape[-1]
        t_mm = op("tensor", lambda e, r=row_ap, n=n: nc.tensor.matmul(
            pc[:, 0:n], c_ones1[:], r, start=True, stop=True),
            s_mm, [pc_free, row_ready])
        t_cp = op("scalar", lambda e, d=dst_ap, n=n: nc.scalar.copy(d, pc[:, 0:n]),
                  s_ac, [(s_mm, t_mm)] + list(extra))
        pc_free = (s_ac, t_cp)
        return t_cp

    for p in range(NPAN):
        t0 = p * T_PAN
        w = [(s_mm, last_xpan_use)] if p > 0 else [(s_ve, a0_used[7]), (s_ac, t_a0_ac)]
        src = dXT.rearrange("(c p) t -> p c t", p=128)
        for blk in range(4):
            pan_loaded[p] = dma(
                "gpsimd", xpan[:, blk * 8:(blk + 1) * 8, :],
                src[:, blk * 8:(blk + 1) * 8, t0:t0 + T_PAN], s_cast,
                w if blk == 0 else ())
        t_cos = dma("sync", cosbuf[:], dCOSK[:, t0:t0 + T_PAN], s_ld,
                    [kaff_free[0], kaff_free[1]] if p > 0 else [])
        t_sin = dma("sync", sinbuf[:], dSINK[:, t0:t0 + T_PAN], s_ld)

        for tq in range(4):
            sb = tq % 2
            for c in range(32):
                xp = x2par
                x2par ^= 1
                t_sq = op("scalar", lambda e, c=c, tq=tq, xp=xp: nc.scalar.activation(
                    x2buf[xp][:, 0:512], xpan[:, c, tq * 512:(tq + 1) * 512], AF.Square),
                    s_ac, [(s_cast, pan_loaded[p]), x2_free[xp]])
                def statmm(e, c=c, tq=tq, sb=sb, xp=xp):
                    nc.tensor.matmul(pst[sb][0:1, :], c_ones_bf[:],
                                     xpan[:, c, tq * 512:(tq + 1) * 512],
                                     start=(c == 0), stop=(c == 31))
                    return nc.tensor.matmul(pst[2 + sb][0:1, :], c_ones_bf[:],
                                            x2buf[xp][:, 0:512],
                                            start=(c == 0), stop=(c == 31))
                t_smm = op("tensor", statmm, s_mm, [(s_ac, t_sq)])
                x2_free[xp] = (s_mm, t_smm)
            t_fin = chain("vector", s_vx, s_ve, [
                lambda e, sb=sb: nc.vector.tensor_scalar_mul(
                    mrow_s[:], pst[sb][0:1, :], 1.0 / D),
                lambda e: nc.vector.tensor_mul(vtmp_s[:], mrow_s[:], mrow_s[:]),
                lambda e, sb=sb: nc.vector.scalar_tensor_tensor(
                    vtmp_s[:], pst[2 + sb][0:1, :], 1.0 / D, vtmp_s[:],
                    ALU.mult, ALU.subtract),
            ], [(s_mm, s_mm.n), mrow_free, rrow_free])
            t_rr = chain("scalar", s_ax, s_ac, [
                lambda e: nc.scalar.activation(
                    vtmp_s[:], vtmp_s[:], AF.Ln, bias=c_eps[0:1, 0:1]),
                lambda e: nc.scalar.activation(
                    rrow_s[:], vtmp_s[:], AF.Exp, scale=-0.5),
            ], [(s_ve, t_fin)])
            t_c1 = row2col(mrow_s[0:1, :], mcol[:, tq * 4:(tq + 1) * 4],
                           [(s_ve, t_fin)])
            t_c3 = row2col(rrow_s[0:1, :], rcol[:, tq * 4:(tq + 1) * 4],
                           [(s_ac, t_rr)])
            t_bm = emit_bcast(mrow_s[0:1, :], mbc[:, tq * 512:(tq + 1) * 512], False,
                              (s_ve, t_fin))
            mrow_free = (s_sh, t_c1)
            t_br = emit_bcast(rrow_s[0:1, :], rbc[:, tq * 512:(tq + 1) * 512], True,
                              (s_ac, t_rr))
            rrow_free = (s_ac, t_br)

        stats_cols = (s_sh, s_sh.n)
        stats_bc = (s_ac, s_ac.n)
        for og in range(NOG):
            wb = og % 2
            w = [wblk_free[wb]]
            if p == 0:
                w.append((s_cast, wcast_tgt[("k", og)]))
            t_wb = dma("sync", wblk[wb][:],
                       dWKB.rearrange("(c p) o -> p c o", p=128)[:, :, og * OG:(og + 1) * OG],
                       s_wb, w)
            if p == 0:
                def csum(e, wb=wb):
                    for c in range(32):
                        ins = nc.tensor.matmul(pc[0:1, 0:OG], c_ones_bf[:],
                                               wblk[wb][:, c, :],
                                               start=(c == 0), stop=(c == 31))
                    return ins
                t_cs = op("tensor", csum, s_mm, [(s_wb, t_wb), pc_free])
                t_neg = op("vector", lambda e: nc.vector.tensor_scalar_mul(
                    crow[:], pc[0:1, 0:OG], -1.0), s_ve,
                    [(s_mm, t_cs), crow_free])
                pc_free = (s_ve, t_neg)
                t_cc = row2col(crow[0:1, :], negc_k[:, og * 2:og * 2 + 2],
                               [(s_ve, t_neg)])
                crow_free = (s_sh, t_cc)
            for oc in range(2):
                hc = og * 2 + oc
                hp_done = 0
                for tq in range(4):
                    zp = zpar
                    zpar ^= 1
                    def zmm(e, wb=wb, oc=oc, tq=tq, zp=zp):
                        for c in range(32):
                            ins = nc.tensor.matmul(
                                pz[zp][:], wblk[wb][:, c, oc * 128:(oc + 1) * 128],
                                xpan[:, c, tq * 512:(tq + 1) * 512],
                                start=(c == 0), stop=(c == 31))
                        return ins
                    zw = [pz_free[zp], (s_wb, t_wb), (s_cast, pan_loaded[p])]
                    if tq == 0 and oc == 0:
                        zw.append(stats_bc)
                        zw.append(stats_cols)
                        if p == 0:
                            zw.append((s_sh, t_cc))
                    t_z = op("tensor", zmm, s_mm, zw)
                    last_xpan_use = t_z
                    hp = tq // 2
                    xp = x2par
                    x2par ^= 1
                    t1v = x2buf[xp][:, 0:1024].bitcast(F32)
                    def evac(e, tq=tq, hc=hc, zp=zp, xp=xp, hp=hp, kp=kpar):
                        t1v_ = x2buf[xp][:, 0:1024].bitcast(F32)
                        nc.vector.scalar_tensor_tensor(
                            t1v_, mbc[:, tq * 512:(tq + 1) * 512],
                            negc_k[:, hc:hc + 1], pz[zp][:], ALU.mult, ALU.add)
                        return nc.vector.tensor_mul(
                            kaff[kp][:, (tq % 2) * 512:((tq % 2) + 1) * 512],
                            t1v_, rbc[:, tq * 512:(tq + 1) * 512])
                    t_e = op("vector", evac, s_ve,
                             [(s_mm, t_z), x2_free[xp], kaff_free[kpar] if tq % 2 == 0 else (s_mm, 0)])
                    pz_free[zp] = (s_ve, t_e)
                    x2_free[xp] = (s_ve, t_e)
                    if tq % 2 == 1:
                        kp = kpar
                        sp = shpar
                        t_s1 = dma("sync", ksh[sp][0:64, :], kaff[kp][64:128, :],
                                   s_sh, [(s_ve, t_e), ksh_free[sp]])
                        t_s2 = dma("sync", ksh[sp][64:128, :], kaff[kp][0:64, :],
                                   s_sh)
                        t_g = op("gpsimd", lambda e, sp=sp, hp=hp: nc.gpsimd.tensor_mul(
                            ksh[sp][:], ksh[sp][:], sinbuf[:, hp * 1024:(hp + 1) * 1024]),
                            s_gp, [(s_sh, t_s2), (s_ld, t_sin)])
                        def ropefin(e, kp=kp, sp=sp, hp=hp):
                            nc.vector.tensor_mul(kaff[kp][:], kaff[kp][:],
                                                 cosbuf[:, hp * 1024:(hp + 1) * 1024])
                            return nc.vector.tensor_add(kaff[kp][:], kaff[kp][:],
                                                        ksh[sp][:])
                        t_rf = op("vector", ropefin, s_ve,
                                  [(s_sh, t_s2), (s_gp, t_g), (s_ld, t_cos)])
                        ksh_free[sp] = (s_ve, t_rf)
                        t_ko = dma("sync", dKG[hc * 128:(hc + 1) * 128,
                                               t0 + hp * 1024:t0 + (hp + 1) * 1024],
                                   kaff[kp][:], s_out, [(s_ve, t_rf)])
                        kaff_free[kp] = (s_out, t_ko)
                        kpar ^= 1
                        shpar ^= 1
            wblk_free[wb] = (s_mm, s_mm.n)

        for og in range(NOG):
            wb = og % 2
            w = [wblk_free[wb]]
            if p == 0:
                w.append((s_cast, wcast_tgt[("v", og)]))
            t_wb = dma("sync", wblk[wb][:],
                       dWVB.rearrange("(c p) o -> p c o", p=128)[:, :, og * OG:(og + 1) * OG],
                       s_wb, w)
            if p == 0:
                def csumv(e, wb=wb):
                    for c in range(32):
                        ins = nc.tensor.matmul(pc[0:1, 0:OG], c_ones_bf[:],
                                               wblk[wb][:, c, :],
                                               start=(c == 0), stop=(c == 31))
                    return ins
                t_cs = op("tensor", csumv, s_mm, [(s_wb, t_wb), pc_free])
                t_neg = op("vector", lambda e: nc.vector.tensor_scalar_mul(
                    crow[:], pc[0:1, 0:OG], -1.0), s_ve,
                    [(s_mm, t_cs), crow_free])
                pc_free = (s_ve, t_neg)
                t_cc = row2col(crow[0:1, :], negc_v[:, og * 2:og * 2 + 2],
                               [(s_ve, t_neg)])
                crow_free = (s_sh, t_cc)
            t_cv = col2row(negc_v[:, og * 2:og * 2 + 2], cvrow[0:1, :],
                           ([(s_sh, t_cc)] if p == 0 else []) + [pc_free])
            t_cvb = emit_bcast(cvrow[0:1, :], cvbc[:], True, (s_sh, t_cv),
                               extra=[cvbc_free])
            for tt in range(16):
                zp = zpar
                zpar ^= 1
                def vmm(e, wb=wb, tt=tt, zp=zp):
                    for c in range(32):
                        ins = nc.tensor.matmul(
                            pz[zp][:, 0:OG], xpan[:, c, tt * 128:(tt + 1) * 128],
                            wblk[wb][:, c, :], start=(c == 0), stop=(c == 31))
                    return ins
                t_z = op("tensor", vmm, s_mm,
                         [pz_free[zp], (s_wb, t_wb), (s_cast, pan_loaded[p]),
                          (s_ac, t_cvb)])
                last_xpan_use = t_z
                vp = vpar
                vpar ^= 1
                xp = x2par
                x2par ^= 1
                def vevac(e, tt=tt, zp=zp, vp=vp, xp=xp):
                    t1v_ = x2buf[xp][:, 0:512].bitcast(F32)
                    nc.vector.scalar_tensor_tensor(
                        t1v_, cvbc[:], mcol[:, tt:tt + 1], pz[zp][:, 0:OG],
                        ALU.mult, ALU.add)
                    return nc.vector.tensor_scalar_mul(
                        vout[vp][:], t1v_, 1.0) if False else nc.vector.tensor_scalar(
                        vout[vp][:], t1v_, rcol[:, tt:tt + 1], None, ALU.mult)
                t_e = op("vector", vevac, s_ve,
                         [(s_mm, t_z), vout_free[vp], x2_free[xp]])
                pz_free[zp] = (s_ve, t_e)
                x2_free[xp] = (s_ve, t_e)
                t_vo = dma("sync", dVG[t0 + tt * 128:t0 + (tt + 1) * 128,
                                       og * OG:(og + 1) * OG],
                           vout[vp][:], s_out, [(s_ve, t_e)])
                vout_free[vp] = (s_out, t_vo)
            cvbc_free = (s_ve, s_ve.n)
            wblk_free[wb] = (s_mm, s_mm.n)

    sxq = arena[:, 0:32768].rearrange("p (c t) -> p c t", c=32)
    src = dXOT.rearrange("(c p) t -> p c t", p=128)
    t_sxq = 0
    for blk in range(4):
        t_sxq = dma("gpsimd", sxq[:, blk * 8:(blk + 1) * 8, :],
                    src[:, blk * 8:(blk + 1) * 8, :], s_cast,
                    [(s_mm, last_xpan_use)] if blk == 0 else [])
    mo_row = arena[0:1, 32768:34816].bitcast(F32)
    ro_row = arena[0:1, 34816:36864].bitcast(F32)
    t_morow = col2row(mo_col[:, :], mo_row,
                      [(s_ve, a0_used[7]), (s_mm, last_xpan_use)])
    t_rorow = col2row(ro_col[:, :], ro_row, [(s_ac, t_a0_ac)])
    t_cosq = dma("sync", cosbuf[:, 0:1024], dCOSQ, s_ld, [kaff_free[0], kaff_free[1]])
    t_sinq = dma("sync", sinbuf[:, 0:1024], dSINQ, s_ld)
    aq_war = (s_ve, s_ve.n)
    t_bm0 = emit_bcast(mo_row[:, 0:512], mbc[:, 0:512], False, (s_sh, t_morow),
                       extra=[aq_war])
    t_bm1 = emit_bcast(mo_row[:, 512:1024], mbc[:, 512:1024], False, (s_sh, t_morow))
    t_br0 = emit_bcast(ro_row[:, 0:512], rbc[:, 0:512], True, (s_sh, t_rorow),
                       extra=[aq_war])
    t_br1 = emit_bcast(ro_row[:, 512:1024], rbc[:, 512:1024], True, (s_sh, t_rorow))
    t_qbc = (s_ac, s_ac.n)
    for og in range(NOG):
        wb = og % 2
        t_wb = dma("gpsimd", wblk[wb][:],
                   dWQ.rearrange("(c p) o -> p c o", p=128)[:, :, og * OG:(og + 1) * OG],
                   s_cast, [wblk_free[wb]])
        def csq(e, wb=wb):
            for c in range(32):
                ins = nc.tensor.matmul(pc[0:1, 0:OG], c_ones_bf[:], wblk[wb][:, c, :],
                                       start=(c == 0), stop=(c == 31))
            return ins
        t_cs = op("tensor", csq, s_mm, [(s_cast, t_wb), pc_free])
        t_neg = op("vector", lambda e: nc.vector.tensor_scalar_mul(
            crow[:], pc[0:1, 0:OG], -1.0), s_ve, [(s_mm, t_cs), crow_free])
        pc_free = (s_ve, t_neg)
        t_cc = row2col(crow[0:1, :], negc_q[:, og * 2:og * 2 + 2],
                       [(s_ve, t_neg)])
        crow_free = (s_sh, t_cc)
        for oc in range(2):
            hc = og * 2 + oc
            kp = kpar
            for tq in range(2):
                zp = zpar
                zpar ^= 1
                def qmm(e, wb=wb, oc=oc, tq=tq, zp=zp):
                    for c in range(32):
                        ins = nc.tensor.matmul(
                            pz[zp][:], wblk[wb][:, c, oc * 128:(oc + 1) * 128],
                            sxq[:, c, tq * 512:(tq + 1) * 512],
                            start=(c == 0), stop=(c == 31))
                    return ins
                t_z = op("tensor", qmm, s_mm,
                         [pz_free[zp], (s_cast, t_wb), (s_cast, t_sxq), t_qbc,
                          (s_sh, t_cc)])
                xp = x2par
                x2par ^= 1
                def qevac(e, tq=tq, hc=hc, zp=zp, xp=xp, kp=kp):
                    t1v_ = x2buf[xp][:, 0:1024].bitcast(F32)
                    nc.vector.scalar_tensor_tensor(
                        t1v_, mbc[:, tq * 512:(tq + 1) * 512],
                        negc_q[:, hc:hc + 1], pz[zp][:], ALU.mult, ALU.add)
                    return nc.vector.tensor_mul(
                        kaff[kp][:, tq * 512:(tq + 1) * 512], t1v_,
                        rbc[:, tq * 512:(tq + 1) * 512])
                t_e = op("vector", qevac, s_ve,
                         [(s_mm, t_z), x2_free[xp],
                          kaff_free[kp] if tq == 0 else (s_mm, 0)])
                pz_free[zp] = (s_ve, t_e)
                x2_free[xp] = (s_ve, t_e)
            sp = shpar
            t_s1 = dma("sync", ksh[sp][0:64, :], kaff[kp][64:128, :], s_sh,
                       [(s_ve, t_e), ksh_free[sp]])
            t_s2 = dma("sync", ksh[sp][64:128, :], kaff[kp][0:64, :], s_sh)
            t_g = op("gpsimd", lambda e, sp=sp: nc.gpsimd.tensor_mul(
                ksh[sp][:], ksh[sp][:], sinbuf[:, 0:1024]),
                s_gp, [(s_sh, t_s2), (s_ld, t_sinq)])
            def ropefinq(e, kp=kp, sp=sp):
                nc.vector.tensor_mul(kaff[kp][:], kaff[kp][:], cosbuf[:, 0:1024])
                return nc.vector.tensor_add(kaff[kp][:], kaff[kp][:], ksh[sp][:])
            t_rf = op("vector", ropefinq, s_ve,
                      [(s_sh, t_s2), (s_gp, t_g), (s_ld, t_cosq)])
            ksh_free[sp] = (s_ve, t_rf)
            t_ko = dma("sync", dQG[hc * 128:(hc + 1) * 128, :], kaff[kp][:],
                       s_out, [(s_ve, t_rf)])
            kaff_free[kp] = (s_out, t_ko)
            kpar ^= 1
            shpar ^= 1
        wblk_free[wb] = (s_mm, s_mm.n)

    srwt = ar_bf(37376, 256).rearrange("p (c e) -> p c e", c=32)
    sAal = ar_bf(37888, 4096).rearrange("p (c r) -> p c r", c=32)
    sBal = ar_bf(41984, 4096)
    sfgt = [ar_f32(46080, 256), ar_f32(46592, 256)]
    b_t1 = ar_f32(47104, 128)
    b_nca_bc = ar_f32(47360, 128)
    b_midwT = ar_bf(47616, 1024)
    b_ffn_war = (s_ac, s_ac.n)
    t_rwt = dma("gpsimd", srwt, dRWT.rearrange("(c p) e -> p c e", p=128),
                s_cast, [(s_mm, s_mm.n), b_ffn_war])
    t_aal = dma("gpsimd", sAal, dAall.rearrange("(c p) r -> p c r", p=128), s_cast)
    t_bal = dma("gpsimd", sBal, dBall, s_cast)
    def csr(e):
        for c in range(32):
            ins = nc.tensor.matmul(pc[0:1, 0:8], c_ones_bf[:], srwt[:, c, :],
                                   start=(c == 0), stop=(c == 31))
        return ins
    t_cs = op("tensor", csr, s_mm, [(s_cast, t_rwt), pc_free])
    t_neg = op("vector", lambda e: nc.vector.tensor_scalar_mul(
        b_ncr[0:1, 0:8], pc[0:1, 0:8], -1.0), s_ve, [(s_mm, t_cs)])
    pc_free = (s_ve, t_neg)
    t_bcr = emit_bcast(b_ncr[0:1, 0:8], b_ncr_bc[:], True, (s_ve, t_neg))
    def csa(e):
        for c in range(32):
            ins = nc.tensor.matmul(pc[0:1, 0:128], c_ones_bf[:], sAal[:, c, :],
                                   start=(c == 0), stop=(c == 31))
        return ins
    t_cs2 = op("tensor", csa, s_mm, [(s_cast, t_aal), pc_free])
    t_neg2 = op("vector", lambda e: nc.vector.tensor_scalar_mul(
        b_ncr[0:1, :], pc[0:1, 0:128], -1.0), s_ve, [(s_mm, t_cs2), (s_ac, t_bcr)])
    pc_free = (s_ve, t_neg2)
    t_bca = emit_bcast(b_ncr[0:1, :], b_nca_bc, True, (s_ve, t_neg2))

    ptr_free = (s_ve, 0)
    midw_free = (s_mm, 0)
    midwT_used = 0
    for tt in range(8):
        zp = zpar
        zpar ^= 1
        def rmm(e, tt=tt, zp=zp):
            for c in range(32):
                ins = nc.tensor.matmul(pz[zp][:, 0:8], sxq[:, c, tt * 128:(tt + 1) * 128],
                                       srwt[:, c, :], start=(c == 0), stop=(c == 31))
            return ins
        t_z = op("tensor", rmm, s_mm, [pz_free[zp], (s_cast, t_rwt), (s_cast, t_sxq)])
        t_r1 = chain("vector", s_vx, s_ve, [
            lambda e, tt=tt, zp=zp: nc.vector.scalar_tensor_tensor(
                b_lg[:], b_ncr_bc[:], mo_col[:, tt:tt + 1],
                pz[zp][:, 0:8], ALU.mult, ALU.add),
            lambda e, tt=tt: nc.vector.tensor_scalar(
                b_lg[:], b_lg[:], ro_col[:, tt:tt + 1], None, ALU.mult),
            lambda e: nc.vector.reduce_max(b_m1[:], b_lg[:], axis=AX.X),
            lambda e: nc.vector.tensor_scalar_mul(b_negm1[:], b_m1[:], -1.0),
            lambda e: nc.vector.tensor_single_scalar(
                b_eq[:], b_lg[:], b_m1[:], ALU.is_equal),
            lambda e: nc.vector.scalar_tensor_tensor(
                b_msk[:], b_eq[:], -1e30, b_lg[:], ALU.mult, ALU.add),
            lambda e: nc.vector.reduce_max(b_m2[:], b_msk[:], axis=AX.X),
        ], [(s_mm, t_z), (s_ac, t_bca)])
        pz_free[zp] = (s_ve, t_r1)
        t_s1 = op("scalar", lambda e: nc.scalar.activation(
            b_s1[:], b_lg[:], AF.Exp, bias=b_negm1[:]), s_ac, [(s_ve, t_r1)])
        t_r2 = chain("vector", s_vx, s_ve, [
            lambda e: nc.vector.tensor_single_scalar(
                b_eq[:], b_lg[:], b_m2[:], ALU.is_ge),
            lambda e: nc.vector.tensor_mul(b_s1[:], b_s1[:], b_eq[:]),
            lambda e: nc.vector.reduce_sum(b_den[:], b_s1[:], axis=AX.X),
            lambda e: nc.vector.reciprocal(b_winv[:], b_den[:]),
            lambda e: nc.vector.tensor_scalar(
                b_s1[:], b_s1[:], b_winv[:], None, ALU.mult),
            lambda e, tt=tt: nc.vector.tensor_scalar(
                b_wro[:, tt, :], b_s1[:], ro_col[:, tt:tt + 1], None, ALU.mult),
        ], [(s_ac, t_s1)])
        zp = zpar
        zpar ^= 1
        def amm(e, tt=tt, zp=zp):
            for c in range(32):
                ins = nc.tensor.matmul(pz[zp][:, 0:128], sxq[:, c, tt * 128:(tt + 1) * 128],
                                       sAal[:, c, :], start=(c == 0), stop=(c == 31))
            return ins
        t_za = op("tensor", amm, s_mm, [pz_free[zp], (s_cast, t_aal)])
        def midw2(e, tt=tt):
            win = b_wro[:, tt, :].unsqueeze(2).broadcast_to([128, 8, 16])
            return nc.vector.tensor_tensor(
                b_midw[:].rearrange("p (e r) -> p e r", e=8),
                b_t1.rearrange("p (e r) -> p e r", e=8), win, ALU.mult)
        t_mw = chain("vector", s_vx, s_ve, [
            lambda e, tt=tt, zp=zp: nc.vector.scalar_tensor_tensor(
                b_t1, b_nca_bc, mo_col[:, tt:tt + 1],
                pz[zp][:, 0:128], ALU.mult, ALU.add),
            midw2,
        ], [(s_mm, t_za), (s_ve, t_r2), midw_free, ptr_free])
        pz_free[zp] = (s_ve, t_mw)
        t_tr = op("tensor", lambda e: nc.tensor.transpose(ptr[:], b_midw[:], c_id[:]),
                  s_mm, [(s_ve, t_mw), (s_ld, t_id), ptr_free])
        midw_free = (s_mm, t_tr)
        t_cpt = op("vector", lambda e, tt=tt: nc.vector.tensor_copy(
            b_midwT[:, tt * 128:(tt + 1) * 128], ptr[:]), s_ve, [(s_mm, t_tr)])
        ptr_free = (s_ve, t_cpt)
        for od in range(8):
            zp = zpar
            zpar ^= 1
            t_zi = op("tensor", lambda e, tt=tt, od=od, zp=zp: nc.tensor.matmul(
                pz[zp][:], b_midwT[:, tt * 128:(tt + 1) * 128],
                sBal[:, od * 512:(od + 1) * 512], start=True, stop=True),
                s_mm, [pz_free[zp], (s_ve, t_cpt), (s_cast, t_bal)])
            xp = x2par
            x2par ^= 1
            def ievac(e, zp=zp, xp=xp):
                t1v_ = x2buf[xp][:, 0:1024].bitcast(F32)
                return nc.vector.tensor_scalar_mul(t1v_, pz[zp][:], 2.0)
            t_ie = op("vector", ievac, s_ve, [(s_mm, t_zi), x2_free[xp]])
            pz_free[zp] = (s_ve, t_ie)
            t_fo = dma("sync", dFG[tt * 128:(tt + 1) * 128, od * 512:(od + 1) * 512],
                       x2buf[xp][:, 0:1024].bitcast(F32), s_out, [(s_ve, t_ie)])
            x2_free[xp] = (s_out, t_fo)
    midwT_used = s_mm.n
    fg_written = s_out.n
    fgt_free = [(s_ve, 0), (s_ve, 0)]
    fpar = 0
    for og in range(NOG):
        wb = og % 2
        t_wb = dma("gpsimd", wblk[wb][:],
                   dBWT.rearrange("(c p) o -> p c o", p=128)[:, :, og * OG:(og + 1) * OG],
                   s_cast, [wblk_free[wb]])
        def csb(e, wb=wb):
            for c in range(32):
                ins = nc.tensor.matmul(pc[0:1, 0:OG], c_ones_bf[:], wblk[wb][:, c, :],
                                       start=(c == 0), stop=(c == 31))
            return ins
        t_cs = op("tensor", csb, s_mm, [(s_cast, t_wb), pc_free])
        t_neg = op("vector", lambda e: nc.vector.tensor_scalar_mul(
            crow[:], pc[0:1, 0:OG], -1.0), s_ve, [(s_mm, t_cs), crow_free])
        pc_free = (s_ve, t_neg)
        t_cvb = emit_bcast(crow[0:1, :], cvbc[:], True, (s_ve, t_neg),
                           extra=[cvbc_free])
        crow_free = pc_free
        for tt in range(8):
            zp = zpar
            zpar ^= 1
            def bmm(e, wb=wb, tt=tt, zp=zp):
                for c in range(32):
                    ins = nc.tensor.matmul(pz[zp][:, 0:OG],
                                           sxq[:, c, tt * 128:(tt + 1) * 128],
                                           wblk[wb][:, c, :], start=(c == 0), stop=(c == 31))
                return ins
            t_z = op("tensor", bmm, s_mm, [pz_free[zp], (s_cast, t_wb), (s_ac, t_cvb)])
            wblk_free[wb] = (s_mm, t_z)
            fp = fpar
            fpar ^= 1
            t_fl = dma("sync", sfgt[fp], dFG[tt * 128:(tt + 1) * 128, og * OG:(og + 1) * OG],
                       s_ld, [fgt_free[fp], (s_out, fg_written)])
            xp = x2par
            x2par ^= 1
            def bevac(e, tt=tt, zp=zp, fp=fp, xp=xp):
                t1v_ = x2buf[xp][:, 0:512].bitcast(F32)
                nc.vector.scalar_tensor_tensor(t1v_, cvbc[:], mo_col[:, tt:tt + 1],
                                               pz[zp][:, 0:OG], ALU.mult, ALU.add)
                return nc.vector.scalar_tensor_tensor(
                    sfgt[fp], t1v_, ro_col[:, tt:tt + 1], sfgt[fp], ALU.mult, ALU.add)
            t_be = op("vector", bevac, s_ve, [(s_mm, t_z), (s_ld, t_fl), x2_free[xp]])
            pz_free[zp] = (s_ve, t_be)
            x2_free[xp] = (s_ve, t_be)
            t_fo = dma("sync", dFG[tt * 128:(tt + 1) * 128, og * OG:(og + 1) * OG],
                       sfgt[fp], s_out, [(s_ve, t_be)])
            fgt_free[fp] = (s_out, t_fo)
        cvbc_free = (s_ve, s_ve.n)
    b_last_mm = s_mm.n
    b_last_out = s_out.n

    mA = ar_bf(0, 16384).rearrange("p (n q) -> p n q", n=NTA)
    mB = ar_bf(16384, 32768).rearrange("p (n q) -> p n q", n=NTB)
    Kh = [ar_bf(49152, 8192), ar_bf(57344, 8192)]
    Vh = [wblk[0][:].rearrange("p a b -> p (a b)").rearrange("p (n d) -> p n d", d=128),
          wblk[1][:].rearrange("p a b -> p (a b)").rearrange("p (n d) -> p n d", d=128)]
    qh = [kaff[0][:], kaff[1][:]]
    PT = ([mbc[:, i * 512:(i + 1) * 512] for i in range(4)]
          + [cosbuf[:, 0:512], cosbuf[:, 512:1024], cosbuf[:, 1024:1536],
             cosbuf[:, 1536:2048]])
    linv = x2buf[0][:, 0:1024].bitcast(F32)
    lbc = [rbc[:, 0:512], rbc[:, 512:1024]]
    aout = [ksh[0][:, 0:512], ksh[0][:, 512:1024]]

    t_mA = dma("sync", mA, dMA, s_ld, [(s_mm, b_last_mm), (s_out, b_last_out)])
    t_mB = dma("sync", mB, dMB, s_ld)
    vg_r = dVG.rearrange("(n p) d -> p n d", p=128)

    c_start_ve = (s_ve, s_ve.n)
    c_start_ld = (s_ld, s_ld.n)
    kh_free = [c_start_ve, c_start_ve]
    vh_free = [(s_mm, b_last_mm), (s_mm, b_last_mm)]
    qh_free = [kaff_free[0], kaff_free[1]]
    pt_free = [c_start_ve] * 8
    ps_free = {id(pz[0]): pz_free[0], id(pz[1]): pz_free[1],
               id(pst[0]): c_start_ve, id(pst[1]): c_start_ve,
               id(pst[2]): c_start_ve, id(pst[3]): c_start_ve}
    po_free = [c_start_ve, c_start_ve]
    pl_free = [c_start_ve, c_start_ve]
    lbc_free = [c_start_ve, c_start_ve]
    aout_free = [ksh_free[0], ksh_free[0]]

    kload = [0, 0]
    vload = [0, 0]
    qload = [0, 0]
    sctr = 0
    ptctr = 0
    ints_written = (s_out, s_out.n)
    for h in range(H):
        par = h % 2
        kload[par] = dma("sync", Kh[par], dKG[h * 128:(h + 1) * 128, :], s_ld,
                         [kh_free[par]] + ([ints_written] if h == 0 else []))
        qload[par] = dma("sync", qh[par], dQG[h * 128:(h + 1) * 128, :], s_ld,
                         [qh_free[par]])
        for j4 in range(4):
            vload[par] = dma("scalar", Vh[par][:, j4 * 16:(j4 + 1) * 16, :],
                             vg_r[:, j4 * 16:(j4 + 1) * 16, h * 128:(h + 1) * 128],
                             s_sh, ([vh_free[par]] + ([ints_written] if h == 0 else []))
                             if j4 == 0 else [])
        for (cidx, ntiles, qoff, po, pl, msk) in (
                (0, NTA, 0, 0, 2, mA), (1, NTB, 512, 1, 3, mB)):
            sbanks = [pz[0], pz[1], pst[1 - po], pst[2 + (1 - po)]]
            DELAY = 2
            pend = {}
            for j in range(ntiles + DELAY):
                if j < ntiles:
                    sb = sbanks[j % 4]
                    ptp = ptctr % 8
                    ptctr += 1
                    t_smm = op("tensor",
                               lambda e, par=par, j=j, qoff=qoff, sb=sb:
                               nc.tensor.matmul(sb[:], Kh[par][:, j * 128:(j + 1) * 128],
                                                qh[par][:, qoff:qoff + 512],
                                                start=True, stop=True),
                               s_mm, [ps_free[id(sb)], (s_ld, kload[par]),
                                      (s_ld, qload[par])])
                    t_exp = op("scalar", lambda e, sb=sb, ptp=ptp: nc.scalar.activation(
                        PT[ptp], sb[:], AF.Exp, bias=c_negC[:], scale=SM_SCALE),
                        s_ac, [(s_mm, t_smm), pt_free[ptp], (s_gp, t_consts)])
                    ps_free[id(sb)] = (s_ac, t_exp)
                    t_msk = op("vector", lambda e, ptp=ptp, msk=msk, j=j:
                               nc.vector.tensor_mul(PT[ptp], PT[ptp], msk[:, j, :]),
                               s_ve, [(s_ac, t_exp), (s_ld, t_mB)])
                    pend[j] = (ptp, t_msk)
                if j >= DELAY:
                    jj = j - DELAY
                    ptp2, t_msk2 = pend.pop(jj)
                    def lpv(e, par=par, jj=jj, po=po, ptp=ptp2,
                            st=(jj == 0), sp_=(jj == ntiles - 1)):
                        nc.tensor.matmul(pst[2 + po][0:1, :], c_ones_bf[:], PT[ptp],
                                         start=st, stop=sp_)
                        return nc.tensor.matmul(pst[po][:], Vh[par][:, jj, :], PT[ptp],
                                                start=st, stop=sp_)
                    w = [(s_ve, t_msk2), (s_sh, vload[par])]
                    if jj == 0:
                        w += [po_free[po], pl_free[po],
                              ps_free[id(pst[po])], ps_free[id(pst[2 + po])]]
                    t_lpv = op("tensor", lpv, s_mm, w)
                    pt_free[ptp2] = (s_mm, t_lpv)
            t_rec = op("vector", lambda e, pl=pl: nc.vector.reciprocal(
                linv[0:1, :], pst[pl][0:1, :]), s_ve, [(s_mm, t_lpv)])
            t_bc = op("tensor", lambda e: nc.tensor.matmul(
                pc[:], c_ones1[:], linv[0:1, :], start=True, stop=True),
                s_mm, [(s_ve, t_rec), pc_free])
            t_cp = op("scalar", lambda e, cidx=cidx: nc.scalar.copy(
                lbc[cidx], pc[:]), s_ac, [(s_mm, t_bc), lbc_free[cidx]])
            pc_free = (s_ac, t_cp)
            def fin(e, po=po, cidx=cidx):
                return nc.vector.tensor_mul(aout[cidx], pst[po][:], lbc[cidx])
            t_fin = op("vector", fin, s_ve,
                       [(s_ac, t_cp), aout_free[cidx]])
            po_free[po] = (s_ve, t_fin)
            pl_free[po] = (s_ve, t_rec)
            ps_free[id(pst[po])] = (s_ve, t_fin)
            ps_free[id(pst[2 + po])] = (s_ve, t_rec)
            lbc_free[cidx] = (s_ve, t_fin)
            t_ao = dma("sync", dAG[h * 128:(h + 1) * 128, qoff:qoff + 512],
                       aout[cidx], s_out, [(s_ve, t_fin)])
            aout_free[cidx] = (s_out, t_ao)
        kh_free[par] = (s_mm, s_mm.n)
        vh_free[par] = (s_mm, s_mm.n)
        qh_free[par] = (s_mm, s_mm.n)
    c_last_mm = s_mm.n
    c_last_gp = s_gp.n
    c_last_out = s_out.n

    sag = arena[:, 0:32768].rearrange("p (c t) -> p c t", c=32)
    woblk = [arena[:, 32768:49152].rearrange("p (c o) -> p c o", c=32),
             arena[:, 49152:65536].rearrange("p (c o) -> p c o", c=32)]
    sxot = [x2buf[0][:, 0:1024].bitcast(F32), x2buf[1][:, 0:1024].bitcast(F32)]
    sfgt2 = [rbc[:, 0:512], rbc[:, 512:1024]]
    syb = [rbc[:, 1024:1536], rbc[:, 1536:2048]]

    t_ag = dma("sync", sag, dAG.rearrange("(c p) t -> p c t", p=128), s_ld,
               [(s_gp, c_last_gp), (s_mm, c_last_mm), (s_out, c_last_out)])
    wo_free = [(s_mm, c_last_mm), (s_mm, c_last_mm)]
    xot_free = [(s_mm, c_last_mm), (s_mm, c_last_mm)]
    fgt2_free = [(s_ve, s_ve.n), (s_ve, s_ve.n)]
    yb_free = [(s_st, 0), (s_st, 0)]
    dpar = 0
    for og in range(8):
        wb = og % 2
        t_wb = dma("gpsimd", woblk[wb],
                   dWO.rearrange("(c p) o -> p c o", p=128)[:, :, og * 512:(og + 1) * 512],
                   s_cast, [wo_free[wb]])
        for tt in range(8):
            zp = zpar
            zpar ^= 1
            def dmm(e, wb=wb, tt=tt, zp=zp):
                for c in range(32):
                    ins = nc.tensor.matmul(pz[zp][:], sag[:, c, tt * 128:(tt + 1) * 128],
                                           woblk[wb][:, c, :], start=(c == 0), stop=(c == 31))
                return ins
            t_z = op("tensor", dmm, s_mm,
                     [ps_free[id(pz[zp])], (s_cast, t_wb), (s_ld, t_ag)])
            wo_free[wb] = (s_mm, t_z)
            dp = dpar
            dpar ^= 1
            t_xl = dma("sync", sxot[dp][:, 0:512],
                       dXO[tt * 128:(tt + 1) * 128, og * 512:(og + 1) * 512], s_ld,
                       [xot_free[dp]])
            t_fl = dma("sync", sfgt2[dp],
                       dFG[tt * 128:(tt + 1) * 128, og * 512:(og + 1) * 512], s_ld,
                       [fgt2_free[dp]])
            def devac(e, zp=zp, dp=dp):
                nc.vector.tensor_add(syb[dp], pz[zp][:], sfgt2[dp])
                return nc.vector.tensor_add(syb[dp], syb[dp], sxot[dp][:, 0:512])
            t_de = op("vector", devac, s_ve,
                      [(s_mm, t_z), (s_ld, t_xl), (s_ld, t_fl), yb_free[dp]])
            ps_free[id(pz[zp])] = (s_ve, t_de)
            xot_free[dp] = (s_ve, t_de)
            fgt2_free[dp] = (s_ve, t_de)
            t_y = dma("sync", dY[tt * 128:(tt + 1) * 128, og * 512:(og + 1) * 512],
                      syb[dp], s_st, [(s_ve, t_de)])
            yb_free[dp] = (s_st, t_y)

    def final_wait(e):
        e.wait_ge(s_st.h, s_st.n)
    emit("sync", final_wait)

    with nc.Block() as block:
        @block.tensor
        def _(eng):
            for fn in Q["tensor"]:
                fn(eng)

        @block.vector
        def _(eng):
            for fn in Q["vector"]:
                fn(eng)

        @block.scalar
        def _(eng):
            for fn in Q["scalar"]:
                fn(eng)

        @block.gpsimd
        def _(eng):
            for fn in Q["gpsimd"]:
                fn(eng)

        @block.sync
        def _(eng):
            for fn in Q["sync"]:
                fn(eng)

    return nc


def _rope_tables(pos):
    d = np.arange(128)
    inv = 10000.0 ** (-(2.0 * (d % 64)) / 128.0)
    ang = inv[:, None] * pos[None, :].astype(np.float64)
    cos = np.cos(ang).astype(np.float32)
    sin = np.sin(ang).astype(np.float32)
    sin[0:64] *= -1.0
    return cos.astype(ml_dtypes.bfloat16), sin.astype(ml_dtypes.bfloat16)


def _masks(core):
    cA, cB = core, 15 - core
    ql = np.arange(512)
    p = np.arange(128)
    mA = np.zeros((128, NTA, 512), dtype=np.float32)
    mB = np.zeros((128, NTB, 512), dtype=np.float32)
    for j in range(NTA):
        kg = 128 * j + p
        mA[:, j, :] = (kg[:, None] <= (512 * cA + ql)[None, :])
    for j in range(NTB):
        kg = 128 * j + p
        mB[:, j, :] = (kg[:, None] <= (512 * cB + ql)[None, :])
    return mA.astype(ml_dtypes.bfloat16), mB.astype(ml_dtypes.bfloat16)


_NC_CACHE = {}


def kernel(L0_attn_hidden_states, L0_attn_wq, L0_attn_wk, L0_attn_wv, L0_attn_wo,
           L0_attn_kq1, L0_attn_kq2, L0_attn_kk1, L0_attn_kk2, L0_attn_scalingD,
           L0_attn_interaction_k, L0_attn_scalingD2, L0_ffn_base_W, L0_ffn_router_W,
           L0_ffn_lora_A, L0_ffn_lora_B):
    x = np.ascontiguousarray(np.asarray(L0_attn_hidden_states, dtype=np.float32)[0])
    xT = np.ascontiguousarray(x.T)
    wq = np.ascontiguousarray(L0_attn_wq, dtype=np.float32)
    wk = np.ascontiguousarray(L0_attn_wk, dtype=np.float32)
    wv = np.ascontiguousarray(L0_attn_wv, dtype=np.float32)
    wo = np.ascontiguousarray(L0_attn_wo, dtype=np.float32)
    bWT = np.ascontiguousarray(np.asarray(L0_ffn_base_W, dtype=np.float32).T)
    rWT = np.ascontiguousarray(np.asarray(L0_ffn_router_W, dtype=np.float32).T)
    Aall = np.ascontiguousarray(
        np.asarray(L0_ffn_lora_A, dtype=np.float32).transpose(1, 0, 2).reshape(D, 128))
    Ball = np.ascontiguousarray(
        np.asarray(L0_ffn_lora_B, dtype=np.float32).reshape(128, D))
    ident = np.eye(128, dtype=ml_dtypes.bfloat16)
    cosk, sink = _rope_tables(np.arange(S))

    if "nc" not in _NC_CACHE:
        _NC_CACHE["nc"] = build()
    nc = _NC_CACHE["nc"]

    in_maps = []
    for i in range(NCORE):
        cA, cB = i, 15 - i
        rows = np.r_[cA * 512:(cA + 1) * 512, cB * 512:(cB + 1) * 512]
        xo = np.ascontiguousarray(x[rows])
        xoT = np.ascontiguousarray(xT[:, rows])
        cosq, sinq = _rope_tables(rows)
        mA, mB = _masks(i)
        in_maps.append({
            "xT": xT, "xo": xo, "xoT": xoT,
            "wq": wq, "wk": wk, "wv": wv, "wo": wo,
            "bWT": bWT, "rWT": rWT, "Aall": Aall, "Ball": Ball,
            "cosq": cosq, "sinq": sinq, "cosk": cosk, "sink": sink,
            "maskA": mA, "maskB": mB, "ident": ident,
        })
    res = run_bass_kernel_spmd(nc, in_maps, core_ids=list(range(NCORE)))
    y = np.empty((S, D), dtype=np.float32)
    for i in range(NCORE):
        yi = np.asarray(res.results[i]["y"])
        y[i * 512:(i + 1) * 512] = yi[0:512]
        y[(15 - i) * 512:(16 - i) * 512] = yi[512:1024]
    return y.reshape(1, S, D)


# version 04 (final)
# speedup vs baseline: 1.4960x; 1.0254x over previous
import math
import numpy as np
import ml_dtypes
import concourse.bass as bass
import concourse.mybir as mybir
from concourse.bass_utils import run_bass_kernel_spmd

F32 = mybir.dt.float32
BF16 = mybir.dt.bfloat16
AF = mybir.ActivationFunctionType
ALU = mybir.AluOpType
AX = mybir.AxisListType

S = 8192
D = 4096
H = 32
NCORE = 8
T_OWN = 1024
T_PAN = 2048
NPAN = S // T_PAN
OG = 256
NOG = D // OG
EXP_C = 12.0
SM_SCALE = 1.0 / math.sqrt(128.0)
NTA = 32
NTB = 64
EPS = 1e-5


def build(debug=False):
    nc = bass.Bass("TRN2", target_bir_lowering=False)

    dXT = nc.dram_tensor("xT", [D, S], F32, kind="ExternalInput").ap()
    dXO = nc.dram_tensor("xo", [T_OWN, D], F32, kind="ExternalInput").ap()
    dXOT = nc.dram_tensor("xoT", [D, T_OWN], F32, kind="ExternalInput").ap()
    dWQ = nc.dram_tensor("wq", [D, D], F32, kind="ExternalInput").ap()
    dWK = nc.dram_tensor("wk", [D, D], F32, kind="ExternalInput").ap()
    dWV = nc.dram_tensor("wv", [D, D], F32, kind="ExternalInput").ap()
    dWO = nc.dram_tensor("wo", [D, D], F32, kind="ExternalInput").ap()
    dBWT = nc.dram_tensor("bWT", [D, D], F32, kind="ExternalInput").ap()
    dRWT = nc.dram_tensor("rWT", [D, 8], F32, kind="ExternalInput").ap()
    dAall = nc.dram_tensor("Aall", [D, 128], F32, kind="ExternalInput").ap()
    dBall = nc.dram_tensor("Ball", [128, D], F32, kind="ExternalInput").ap()
    dCOSQ = nc.dram_tensor("cosq", [128, T_OWN], BF16, kind="ExternalInput").ap()
    dSINQ = nc.dram_tensor("sinq", [128, T_OWN], BF16, kind="ExternalInput").ap()
    dCOSK = nc.dram_tensor("cosk", [128, S], BF16, kind="ExternalInput").ap()
    dSINK = nc.dram_tensor("sink", [128, S], BF16, kind="ExternalInput").ap()
    dMA = nc.dram_tensor("maskA", [128, NTA, 512], BF16, kind="ExternalInput").ap()
    dMB = nc.dram_tensor("maskB", [128, NTB, 512], BF16, kind="ExternalInput").ap()
    dID = nc.dram_tensor("ident", [128, 128], BF16, kind="ExternalInput").ap()

    dY = nc.dram_tensor("y", [T_OWN, D], F32, kind="ExternalOutput").ap()
    ikind = "ExternalOutput" if debug else "Internal"
    dWKB = nc.dram_tensor("wkb", [D, D], BF16, kind="Internal").ap()
    dWVB = nc.dram_tensor("wvb", [D, D], BF16, kind="Internal").ap()
    dKG = nc.dram_tensor("Kg", [D, S], BF16, kind=ikind).ap()
    dVG = nc.dram_tensor("Vg", [S, D], BF16, kind=ikind).ap()
    dQG = nc.dram_tensor("qg", [D, T_OWN], BF16, kind=ikind).ap()
    dAG = nc.dram_tensor("ag", [D, T_OWN], BF16, kind=ikind).ap()
    dFG = nc.dram_tensor("fg", [T_OWN, D], F32, kind=ikind).ap()
    dSCR = nc.dram_tensor("scr", [256, 1024], F32, kind="Internal").ap()

    A_ = nc.alloc_sbuf_tensor
    c_id = A_("c_id", [128, 128], BF16)
    c_ones_bf = A_("c_ones_bf", [128, 1], BF16)
    c_ones1 = A_("c_ones1", [1, 128], F32)
    c_negC = A_("c_negC", [128, 1], F32)
    c_eps = A_("c_eps", [128, 1], F32)

    arena = A_("arena", [128, 65536], BF16)
    wblk2 = A_("wblk2", [128, 64, OG], BF16)
    wblk = [wblk2[:, 0:32, :], wblk2[:, 32:64, :]]
    vview = wblk2[:].rearrange("p a b -> p (a b)").rearrange(
        "p (c o) -> p c o", o=512)
    x2buf = [A_(f"x2buf{i}", [128, 1024], BF16) for i in range(2)]
    mrow_s = A_("mrow_s", [1, 512], F32)
    rrow_s = A_("rrow_s", [1, 512], F32)
    vtmp_s = A_("vtmp_s", [1, 512], F32)
    mcol = A_("mcol", [128, 16], F32)
    rcol = A_("rcol", [128, 16], F32)
    mbc = A_("mbc", [128, T_PAN], BF16)
    rbc = A_("rbc", [128, T_PAN], F32)
    negc_k = A_("negc_k", [128, 32], F32)
    negc_v = A_("negc_v", [128, 32], F32)
    negc_q = A_("negc_q", [128, 32], F32)
    crow = A_("crow", [1, OG], F32)
    cvrow = A_("cvrow", [1, OG], F32)
    cvbc = A_("cvbc", [128, 512], F32)
    kaff = [A_(f"kaff{i}", [128, 1024], BF16) for i in range(2)]
    ksh = [A_(f"ksh{i}", [128, 1024], BF16) for i in range(2)]
    cosbuf = A_("cosbuf", [128, 2048], BF16)
    sinbuf = A_("sinbuf", [128, 2048], BF16)
    mo_col = A_("mo_col", [128, 8], F32)
    negmo_col = A_("negmo_col", [128, 8], F32)
    ro_col = A_("ro_col", [128, 8], F32)
    stat_s = A_("stat_s", [128, 1], F32)
    stat_ss = A_("stat_ss", [128, 1], F32)
    stat_v = A_("stat_v", [128, 1], F32)
    b_lg = A_("b_lg", [128, 8], F32)
    b_eq = A_("b_eq", [128, 8], F32)
    b_msk = A_("b_msk", [128, 8], F32)
    b_m1 = A_("b_m1", [128, 1], F32)
    b_negm1 = A_("b_negm1", [128, 1], F32)
    b_m2 = A_("b_m2", [128, 1], F32)
    b_s1 = A_("b_s1", [128, 8], F32)
    b_den = A_("b_den", [128, 1], F32)
    b_winv = A_("b_winv", [128, 1], F32)
    b_wro = A_("b_wro", [128, 8, 8], F32)
    b_midw = A_("b_midw", [128, 128], BF16)
    b_ncr = A_("b_ncr", [1, 128], F32)
    b_ncr_bc = A_("b_ncr_bc", [128, 8], F32)

    P_ = nc.alloc_psum_tensor
    pz = [P_(f"pz{i}", [128, 512], F32) for i in range(2)]
    pst = [P_(f"pst{i}", [128, 512], F32) for i in range(4)]
    pc = P_("pc", [128, 512], F32)
    ptr = P_("ptr", [128, 128], BF16)

    ENGS = ("tensor", "vector", "scalar", "gpsimd", "sync")
    Q = {e: [] for e in ENGS}

    class Sem:
        def __init__(self, name):
            self.h = nc.alloc_semaphore(name)
            self.n = 0
    s_ld = Sem("s_ld")
    s_cast = Sem("s_cast")
    s_wb = Sem("s_wb")
    s_sh = Sem("s_sh")
    s_out = Sem("s_out")
    s_mm = Sem("s_mm")
    s_ve = Sem("s_ve")
    s_ac = Sem("s_ac")
    s_gp = Sem("s_gp")
    s_st = Sem("s_st")
    s_vx = Sem("s_vx")
    s_ax = Sem("s_ax")

    def emit(eng, fn):
        Q[eng].append(fn)

    def dma(eng_q, out, in_, sem, waits=(), transpose=False, slow=False):
        sem.n += 16
        tgt = sem.n
        def fn(e, out=out, in_=in_, waits=tuple(waits)):
            for (s, t) in waits:
                e.wait_ge(s.h, t)
            e.dma_start(out=out, in_=in_, transpose=transpose,
                        allow_slow_non_contiguous=slow).then_inc(sem.h, 16)
        emit(eng_q, fn)
        return tgt

    def chain(eng, chain_sem, final_sem, steps, waits):
        for i, fn_ in enumerate(steps[:-1]):
            op(eng, fn_, chain_sem,
               list(waits) if i == 0 else [(chain_sem, chain_sem.n)])
        return op(eng, steps[-1], final_sem,
                  [(chain_sem, chain_sem.n)] if len(steps) > 1 else list(waits))

    scr_slot = [0]

    def row2col(row_ap, col_ap, waits, pp=128):
        sl = scr_slot[0]; scr_slot[0] += 1
        n = row_ap.shape[-1]
        t1 = dma("sync", dSCR[sl, 0:n], row_ap, s_sh, waits)
        t2 = dma("sync", col_ap,
                 dSCR[sl, 0:n].rearrange("(t p) -> p t", p=pp), s_sh, [(s_sh, t1)],
                 slow=True)
        return t2

    def col2row(col_ap, row_ap, waits, pp=128):
        sl = scr_slot[0]; scr_slot[0] += 1
        n = row_ap.shape[-1]
        t1 = dma("sync", dSCR[sl, 0:n].rearrange("(t p) -> p t", p=pp), col_ap,
                 s_sh, waits, slow=True)
        t2 = dma("sync", row_ap, dSCR[sl, 0:n], s_sh, [(s_sh, t1)])
        return t2

    def op(eng, build_fn, sem=None, waits=(), k=1):
        tgt = None
        if sem is not None:
            sem.n += k
            tgt = sem.n
        def fn(e, waits=tuple(waits)):
            for (s, t) in waits:
                e.wait_ge(s.h, t)
            ins = build_fn(e)
            if sem is not None:
                ins.then_inc(sem.h, k)
        emit(eng, fn)
        return tgt

    def ar_bf(lo, n):
        return arena[:, lo:lo + n]
    def ar_f32(lo, n):
        return arena[:, lo:lo + 2 * n].bitcast(F32)

    op("gpsimd", lambda e: (nc.gpsimd.memset(c_ones_bf[:], 1.0),
                            nc.gpsimd.memset(c_ones1[:], 1.0),
                            nc.gpsimd.memset(c_negC[:], -EXP_C),
                            nc.gpsimd.memset(c_eps[:], EPS))[-1], s_gp)
    t_consts = s_gp.n
    t_id = dma("sync", c_id[:], dID, s_ld)

    xo_buf = [ar_f32(0, 4096), ar_f32(8192, 4096)]
    sqdump = ar_f32(16384, 4096)
    a0_ld = [0] * 8
    a0_used = [0] * 8
    for tt in range(8):
        buf = xo_buf[tt % 2]
        w = []
        if tt >= 2:
            w.append((s_ve, a0_used[tt - 2]))
        a0_ld[tt] = dma("sync", buf, dXO[tt * 128:(tt + 1) * 128, :], s_ld, w)
        tt_ac = op("scalar", lambda e, buf=buf: nc.scalar.activation(
            sqdump, buf, AF.Square, accum_out=stat_ss[:]),
            s_ac, [(s_ld, a0_ld[tt])] + ([(s_ve, a0_used[tt - 1])] if tt >= 1 else []))
        a0_used[tt] = chain("vector", s_vx, s_ve, [
            lambda e, buf=buf: nc.vector.reduce_sum(stat_s[:], buf, axis=AX.X),
            lambda e, tt=tt: nc.vector.tensor_scalar_mul(
                mo_col[:, tt:tt + 1], stat_s[:], 1.0 / D),
            lambda e, tt=tt: nc.vector.tensor_scalar_mul(
                negmo_col[:, tt:tt + 1], stat_s[:], -1.0 / D),
            lambda e, tt=tt: nc.vector.tensor_mul(
                stat_v[:], mo_col[:, tt:tt + 1], mo_col[:, tt:tt + 1]),
            lambda e: nc.vector.scalar_tensor_tensor(
                stat_v[:], stat_ss[:], 1.0 / D, stat_v[:], ALU.mult, ALU.subtract),
        ], [(s_ld, a0_ld[tt]), (s_ac, tt_ac)])
        chain("scalar", s_ax, s_ac, [
            lambda e: nc.scalar.activation(stat_v[:], stat_v[:], AF.Ln, bias=c_eps[:]),
            lambda e, tt=tt: nc.scalar.activation(
                ro_col[:, tt:tt + 1], stat_v[:], AF.Exp, scale=-0.5),
        ], [(s_ve, a0_used[tt])])
    t_a0_ac = s_ac.n

    wcast_tgt = {}
    for nm, (src, dst) in (("k", (dWK, dWKB)), ("v", (dWV, dWVB))):
        for og in range(NOG):
            wcast_tgt[(nm, og)] = dma(
                "gpsimd", dst[:, og * OG:(og + 1) * OG], src[:, og * OG:(og + 1) * OG],
                s_cast)

    xpan = arena[:, 0:65536].rearrange("p (c t) -> p c t", c=32)
    pan_loaded = [0] * NPAN
    pz_free = [(s_mm, 0), (s_mm, 0)]
    pzv = [None, None]
    wblk_free = [(s_mm, 0), (s_mm, 0)]
    x2_free = [(s_mm, 0), (s_mm, 0)]
    kaff_free = [(s_out, 0), (s_out, 0)]
    ksh_free = [(s_ve, 0), (s_ve, 0)]
    vout_free = [(s_out, 0), (s_out, 0)]
    pc_free = (s_gp, t_consts)
    mrow_free = (s_mm, 0)
    rrow_free = (s_mm, 0)
    zpar = 0
    kpar = 0
    shpar = 0
    vpar = 0
    x2par = 0
    vslabs = [ksh[0][:, 0:512], ksh[0][:, 512:1024],
              ksh[1][:, 0:512], ksh[1][:, 512:1024]]
    vslab_free = [(s_out, 0)] * 4
    ksh_vout = [(s_out, 0), (s_out, 0)]
    crow_free = (s_mm, 0)
    cvbc_free = (s_ve, 0)

    last_xpan_use = 0

    def emit_bcast(row_ap, dst_ap, dst_f32, row_ready, extra=()):
        nonlocal pc_free
        n = row_ap.shape[-1]
        t_mm = op("tensor", lambda e, r=row_ap, n=n: nc.tensor.matmul(
            pc[:, 0:n], c_ones1[:], r, start=True, stop=True),
            s_mm, [pc_free, row_ready])
        t_cp = op("scalar", lambda e, d=dst_ap, n=n: nc.scalar.copy(d, pc[:, 0:n]),
                  s_ac, [(s_mm, t_mm)] + list(extra))
        pc_free = (s_ac, t_cp)
        return t_cp

    for p in range(NPAN):
        t0 = p * T_PAN
        w = [(s_mm, last_xpan_use)] if p > 0 else [(s_ve, a0_used[7]), (s_ac, t_a0_ac)]
        src = dXT.rearrange("(c p) t -> p c t", p=128)
        for blk in range(4):
            pan_loaded[p] = dma(
                "gpsimd", xpan[:, blk * 8:(blk + 1) * 8, :],
                src[:, blk * 8:(blk + 1) * 8, t0:t0 + T_PAN], s_cast,
                w if blk == 0 else ())
        t_cos = dma("sync", cosbuf[:], dCOSK[:, t0:t0 + T_PAN], s_ld,
                    [kaff_free[0], kaff_free[1]] if p > 0 else [])
        t_sin = dma("sync", sinbuf[:], dSINK[:, t0:t0 + T_PAN], s_ld)

        for tq in range(4):
            sb = tq % 2
            for c in range(32):
                xp = x2par
                x2par ^= 1
                t_sq = op("scalar", lambda e, c=c, tq=tq, xp=xp: nc.scalar.activation(
                    x2buf[xp][:, 0:512], xpan[:, c, tq * 512:(tq + 1) * 512], AF.Square),
                    s_ac, [(s_cast, pan_loaded[p]), x2_free[xp]])
                def statmm(e, c=c, tq=tq, sb=sb, xp=xp):
                    nc.tensor.matmul(pst[sb][0:1, :], c_ones_bf[:],
                                     xpan[:, c, tq * 512:(tq + 1) * 512],
                                     start=(c == 0), stop=(c == 31))
                    return nc.tensor.matmul(pst[2 + sb][0:1, :], c_ones_bf[:],
                                            x2buf[xp][:, 0:512],
                                            start=(c == 0), stop=(c == 31))
                t_smm = op("tensor", statmm, s_mm, [(s_ac, t_sq)])
                x2_free[xp] = (s_mm, t_smm)
            t_fin = chain("vector", s_vx, s_ve, [
                lambda e, sb=sb: nc.vector.tensor_scalar_mul(
                    mrow_s[:], pst[sb][0:1, :], 1.0 / D),
                lambda e: nc.vector.tensor_mul(vtmp_s[:], mrow_s[:], mrow_s[:]),
                lambda e, sb=sb: nc.vector.scalar_tensor_tensor(
                    vtmp_s[:], pst[2 + sb][0:1, :], 1.0 / D, vtmp_s[:],
                    ALU.mult, ALU.subtract),
            ], [(s_mm, s_mm.n), mrow_free, rrow_free])
            t_rr = chain("scalar", s_ax, s_ac, [
                lambda e: nc.scalar.activation(
                    vtmp_s[:], vtmp_s[:], AF.Ln, bias=c_eps[0:1, 0:1]),
                lambda e: nc.scalar.activation(
                    rrow_s[:], vtmp_s[:], AF.Exp, scale=-0.5),
            ], [(s_ve, t_fin)])
            t_c1 = row2col(mrow_s[0:1, :], mcol[:, tq * 4:(tq + 1) * 4],
                           [(s_ve, t_fin)])
            t_c3 = row2col(rrow_s[0:1, :], rcol[:, tq * 4:(tq + 1) * 4],
                           [(s_ac, t_rr)])
            t_bm = emit_bcast(mrow_s[0:1, :], mbc[:, tq * 512:(tq + 1) * 512], False,
                              (s_ve, t_fin))
            mrow_free = (s_sh, t_c1)
            t_br = emit_bcast(rrow_s[0:1, :], rbc[:, tq * 512:(tq + 1) * 512], True,
                              (s_ac, t_rr))
            rrow_free = (s_ac, t_br)

        stats_cols = (s_sh, s_sh.n)
        stats_bc = (s_ac, s_ac.n)
        for og in range(NOG):
            wb = og % 2
            w = [wblk_free[wb]]
            if p == 0:
                w.append((s_cast, wcast_tgt[("k", og)]))
            t_wb = dma("sync", wblk[wb],
                       dWKB.rearrange("(c p) o -> p c o", p=128)[:, :, og * OG:(og + 1) * OG],
                       s_wb, w)
            if p == 0:
                def csum(e, wb=wb):
                    for c in range(32):
                        ins = nc.tensor.matmul(pc[0:1, 0:OG], c_ones_bf[:],
                                               wblk[wb][:, c, :],
                                               start=(c == 0), stop=(c == 31))
                    return ins
                t_cs = op("tensor", csum, s_mm, [(s_wb, t_wb), pc_free])
                t_neg = op("vector", lambda e: nc.vector.tensor_scalar_mul(
                    crow[:], pc[0:1, 0:OG], -1.0), s_ve,
                    [(s_mm, t_cs), crow_free])
                pc_free = (s_ve, t_neg)
                t_cc = row2col(crow[0:1, :], negc_k[:, og * 2:og * 2 + 2],
                               [(s_ve, t_neg)])
                crow_free = (s_sh, t_cc)
            for oc in range(2):
                hc = og * 2 + oc
                hp_done = 0
                for tq in range(4):
                    zp = zpar
                    zpar ^= 1
                    def zmm(e, wb=wb, oc=oc, tq=tq, zp=zp):
                        for c in range(32):
                            ins = nc.tensor.matmul(
                                pz[zp][:], wblk[wb][:, c, oc * 128:(oc + 1) * 128],
                                xpan[:, c, tq * 512:(tq + 1) * 512],
                                start=(c == 0), stop=(c == 31))
                        return ins
                    zw = [pz_free[zp], (s_wb, t_wb), (s_cast, pan_loaded[p])]
                    if tq == 0 and oc == 0:
                        zw.append(stats_bc)
                        zw.append(stats_cols)
                        if p == 0:
                            zw.append((s_sh, t_cc))
                    t_z = op("tensor", zmm, s_mm, zw)
                    last_xpan_use = t_z
                    hp = tq // 2
                    xp = x2par
                    x2par ^= 1
                    t1v = x2buf[xp][:, 0:1024].bitcast(F32)
                    def evac(e, tq=tq, hc=hc, zp=zp, xp=xp, hp=hp, kp=kpar):
                        t1v_ = x2buf[xp][:, 0:1024].bitcast(F32)
                        nc.vector.scalar_tensor_tensor(
                            t1v_, mbc[:, tq * 512:(tq + 1) * 512],
                            negc_k[:, hc:hc + 1], pz[zp][:], ALU.mult, ALU.add)
                        return nc.vector.tensor_mul(
                            kaff[kp][:, (tq % 2) * 512:((tq % 2) + 1) * 512],
                            t1v_, rbc[:, tq * 512:(tq + 1) * 512])
                    t_e = op("vector", evac, s_ve,
                             [(s_mm, t_z), x2_free[xp], kaff_free[kpar] if tq % 2 == 0 else (s_mm, 0)])
                    pz_free[zp] = (s_ve, t_e)
                    x2_free[xp] = (s_ve, t_e)
                    if tq % 2 == 1:
                        kp = kpar
                        sp = shpar
                        t_s1 = dma("sync", ksh[sp][0:64, :], kaff[kp][64:128, :],
                                   s_sh, [(s_ve, t_e), ksh_free[sp], ksh_vout[sp]])
                        t_s2 = dma("sync", ksh[sp][64:128, :], kaff[kp][0:64, :],
                                   s_sh)
                        t_g = op("gpsimd", lambda e, sp=sp, hp=hp: nc.gpsimd.tensor_mul(
                            ksh[sp][:], ksh[sp][:], sinbuf[:, hp * 1024:(hp + 1) * 1024]),
                            s_gp, [(s_sh, t_s2), (s_ld, t_sin)])
                        def ropefin(e, kp=kp, sp=sp, hp=hp):
                            nc.vector.tensor_mul(kaff[kp][:], kaff[kp][:],
                                                 cosbuf[:, hp * 1024:(hp + 1) * 1024])
                            return nc.vector.tensor_add(kaff[kp][:], kaff[kp][:],
                                                        ksh[sp][:])
                        t_rf = op("vector", ropefin, s_ve,
                                  [(s_sh, t_s2), (s_gp, t_g), (s_ld, t_cos)])
                        ksh_free[sp] = (s_ve, t_rf)
                        t_ko = dma("sync", dKG[hc * 128:(hc + 1) * 128,
                                               t0 + hp * 1024:t0 + (hp + 1) * 1024],
                                   kaff[kp][:], s_out, [(s_ve, t_rf)])
                        kaff_free[kp] = (s_out, t_ko)
                        kpar ^= 1
                        shpar ^= 1
            wblk_free[wb] = (s_mm, s_mm.n)

        for i4 in range(4):
            if vslab_free[i4][1] < ksh_free[i4 // 2][1]:
                vslab_free[i4] = ksh_free[i4 // 2]
        for og in range(8):
            w = [(s_mm, s_mm.n)]
            if p == 0:
                w.append((s_cast, wcast_tgt[("v", 2 * og + 1)]))
            t_wb = dma("sync", vview,
                       dWVB.rearrange("(c p) o -> p c o", p=128)[:, :, og * 512:(og + 1) * 512],
                       s_wb, w)
            if p == 0:
                def csumv(e):
                    for c in range(32):
                        ins = nc.tensor.matmul(pc[0:1, 0:512], c_ones_bf[:],
                                               vview[:, c, :],
                                               start=(c == 0), stop=(c == 31))
                    return ins
                t_cs = op("tensor", csumv, s_mm, [(s_wb, t_wb), pc_free])
                t_neg = op("vector", lambda e: nc.vector.tensor_scalar_mul(
                    mrow_s[:], pc[0:1, 0:512], -1.0), s_ve,
                    [(s_mm, t_cs), mrow_free])
                pc_free = (s_ve, t_neg)
                t_cc = row2col(mrow_s[0:1, :], negc_v[:, og * 4:og * 4 + 4],
                               [(s_ve, t_neg)])
                mrow_free = (s_sh, t_cc)
            t_cv = col2row(negc_v[:, og * 4:og * 4 + 4], rrow_s[0:1, :],
                           ([(s_sh, t_cc)] if p == 0 else []) + [pc_free, rrow_free])
            t_cvb = emit_bcast(rrow_s[0:1, :], cvbc[:], True, (s_sh, t_cv),
                               extra=[cvbc_free])
            rrow_free = pc_free
            for tt in range(16):
                zp = zpar
                zpar ^= 1
                def vmm(e, tt=tt, zp=zp):
                    for c in range(32):
                        ins = nc.tensor.matmul(
                            pz[zp][:], xpan[:, c, tt * 128:(tt + 1) * 128],
                            vview[:, c, :], start=(c == 0), stop=(c == 31))
                    return ins
                t_z = op("tensor", vmm, s_mm,
                         [pz_free[zp], (s_wb, t_wb), (s_cast, pan_loaded[p]),
                          (s_ac, t_cvb)])
                last_xpan_use = t_z
                vp = vpar
                vpar = (vpar + 1) % 4
                xp = x2par
                x2par ^= 1
                vslab = ksh[vp // 2][:, (vp % 2) * 512:((vp % 2) + 1) * 512] \
                    if False else vslabs[vp]
                def vevac(e, tt=tt, zp=zp, vp=vp, xp=xp):
                    t1v_ = x2buf[xp][:, 0:1024].bitcast(F32)
                    nc.vector.scalar_tensor_tensor(
                        t1v_, cvbc[:], mcol[:, tt:tt + 1], pz[zp][:],
                        ALU.mult, ALU.add)
                    return nc.vector.tensor_scalar(
                        vslabs[vp], t1v_, rcol[:, tt:tt + 1], None, ALU.mult)
                t_e = op("vector", vevac, s_ve,
                         [(s_mm, t_z), vslab_free[vp], x2_free[xp]])
                pz_free[zp] = (s_ve, t_e)
                x2_free[xp] = (s_ve, t_e)
                t_vo = dma("sync", dVG[t0 + tt * 128:t0 + (tt + 1) * 128,
                                       og * 512:(og + 1) * 512],
                           vslabs[vp], s_out, [(s_ve, t_e)])
                vslab_free[vp] = (s_out, t_vo)
            cvbc_free = (s_ve, s_ve.n)
        ksh_vout[0] = max(vslab_free[0], vslab_free[1], key=lambda t: t[1])
        ksh_vout[1] = max(vslab_free[2], vslab_free[3], key=lambda t: t[1])

    sxq = arena[:, 0:32768].rearrange("p (c t) -> p c t", c=32)
    src = dXOT.rearrange("(c p) t -> p c t", p=128)
    t_sxq = 0
    for blk in range(4):
        t_sxq = dma("gpsimd", sxq[:, blk * 8:(blk + 1) * 8, :],
                    src[:, blk * 8:(blk + 1) * 8, :], s_cast,
                    [(s_mm, last_xpan_use)] if blk == 0 else [])
    mo_row = arena[0:1, 32768:34816].bitcast(F32)
    ro_row = arena[0:1, 34816:36864].bitcast(F32)
    t_morow = col2row(mo_col[:, :], mo_row,
                      [(s_ve, a0_used[7]), (s_mm, last_xpan_use)])
    t_rorow = col2row(ro_col[:, :], ro_row, [(s_ac, t_a0_ac)])
    t_cosq = dma("sync", cosbuf[:, 0:1024], dCOSQ, s_ld, [kaff_free[0], kaff_free[1]])
    t_sinq = dma("sync", sinbuf[:, 0:1024], dSINQ, s_ld)
    aq_war = (s_ve, s_ve.n)
    t_bm0 = emit_bcast(mo_row[:, 0:512], mbc[:, 0:512], False, (s_sh, t_morow),
                       extra=[aq_war])
    t_bm1 = emit_bcast(mo_row[:, 512:1024], mbc[:, 512:1024], False, (s_sh, t_morow))
    t_br0 = emit_bcast(ro_row[:, 0:512], rbc[:, 0:512], True, (s_sh, t_rorow),
                       extra=[aq_war])
    t_br1 = emit_bcast(ro_row[:, 512:1024], rbc[:, 512:1024], True, (s_sh, t_rorow))
    t_qbc = (s_ac, s_ac.n)
    for og in range(NOG):
        wb = og % 2
        t_wb = dma("gpsimd", wblk[wb],
                   dWQ.rearrange("(c p) o -> p c o", p=128)[:, :, og * OG:(og + 1) * OG],
                   s_cast, [wblk_free[wb]])
        def csq(e, wb=wb):
            for c in range(32):
                ins = nc.tensor.matmul(pc[0:1, 0:OG], c_ones_bf[:], wblk[wb][:, c, :],
                                       start=(c == 0), stop=(c == 31))
            return ins
        t_cs = op("tensor", csq, s_mm, [(s_cast, t_wb), pc_free])
        t_neg = op("vector", lambda e: nc.vector.tensor_scalar_mul(
            crow[:], pc[0:1, 0:OG], -1.0), s_ve, [(s_mm, t_cs), crow_free])
        pc_free = (s_ve, t_neg)
        t_cc = row2col(crow[0:1, :], negc_q[:, og * 2:og * 2 + 2],
                       [(s_ve, t_neg)])
        crow_free = (s_sh, t_cc)
        for oc in range(2):
            hc = og * 2 + oc
            kp = kpar
            for tq in range(2):
                zp = zpar
                zpar ^= 1
                def qmm(e, wb=wb, oc=oc, tq=tq, zp=zp):
                    for c in range(32):
                        ins = nc.tensor.matmul(
                            pz[zp][:], wblk[wb][:, c, oc * 128:(oc + 1) * 128],
                            sxq[:, c, tq * 512:(tq + 1) * 512],
                            start=(c == 0), stop=(c == 31))
                    return ins
                t_z = op("tensor", qmm, s_mm,
                         [pz_free[zp], (s_cast, t_wb), (s_cast, t_sxq), t_qbc,
                          (s_sh, t_cc)])
                xp = x2par
                x2par ^= 1
                def qevac(e, tq=tq, hc=hc, zp=zp, xp=xp, kp=kp):
                    t1v_ = x2buf[xp][:, 0:1024].bitcast(F32)
                    nc.vector.scalar_tensor_tensor(
                        t1v_, mbc[:, tq * 512:(tq + 1) * 512],
                        negc_q[:, hc:hc + 1], pz[zp][:], ALU.mult, ALU.add)
                    return nc.vector.tensor_mul(
                        kaff[kp][:, tq * 512:(tq + 1) * 512], t1v_,
                        rbc[:, tq * 512:(tq + 1) * 512])
                t_e = op("vector", qevac, s_ve,
                         [(s_mm, t_z), x2_free[xp],
                          kaff_free[kp] if tq == 0 else (s_mm, 0)])
                pz_free[zp] = (s_ve, t_e)
                x2_free[xp] = (s_ve, t_e)
            sp = shpar
            t_s1 = dma("sync", ksh[sp][0:64, :], kaff[kp][64:128, :], s_sh,
                       [(s_ve, t_e), ksh_free[sp], ksh_vout[sp]])
            t_s2 = dma("sync", ksh[sp][64:128, :], kaff[kp][0:64, :], s_sh)
            t_g = op("gpsimd", lambda e, sp=sp: nc.gpsimd.tensor_mul(
                ksh[sp][:], ksh[sp][:], sinbuf[:, 0:1024]),
                s_gp, [(s_sh, t_s2), (s_ld, t_sinq)])
            def ropefinq(e, kp=kp, sp=sp):
                nc.vector.tensor_mul(kaff[kp][:], kaff[kp][:], cosbuf[:, 0:1024])
                return nc.vector.tensor_add(kaff[kp][:], kaff[kp][:], ksh[sp][:])
            t_rf = op("vector", ropefinq, s_ve,
                      [(s_sh, t_s2), (s_gp, t_g), (s_ld, t_cosq)])
            ksh_free[sp] = (s_ve, t_rf)
            t_ko = dma("sync", dQG[hc * 128:(hc + 1) * 128, :], kaff[kp][:],
                       s_out, [(s_ve, t_rf)])
            kaff_free[kp] = (s_out, t_ko)
            kpar ^= 1
            shpar ^= 1
        wblk_free[wb] = (s_mm, s_mm.n)

    srwt = ar_bf(37376, 256).rearrange("p (c e) -> p c e", c=32)
    sAal = ar_bf(37888, 4096).rearrange("p (c r) -> p c r", c=32)
    sBal = ar_bf(41984, 4096)
    sfgt = [ar_f32(46080, 256), ar_f32(46592, 256)]
    b_t1 = ar_f32(47104, 128)
    b_nca_bc = ar_f32(47360, 128)
    b_midwT = ar_bf(47616, 1024)
    b_ffn_war = (s_ac, s_ac.n)
    t_rwt = dma("gpsimd", srwt, dRWT.rearrange("(c p) e -> p c e", p=128),
                s_cast, [(s_mm, s_mm.n), b_ffn_war])
    t_aal = dma("gpsimd", sAal, dAall.rearrange("(c p) r -> p c r", p=128), s_cast)
    t_bal = dma("gpsimd", sBal, dBall, s_cast)
    def csr(e):
        for c in range(32):
            ins = nc.tensor.matmul(pc[0:1, 0:8], c_ones_bf[:], srwt[:, c, :],
                                   start=(c == 0), stop=(c == 31))
        return ins
    t_cs = op("tensor", csr, s_mm, [(s_cast, t_rwt), pc_free])
    t_neg = op("vector", lambda e: nc.vector.tensor_scalar_mul(
        b_ncr[0:1, 0:8], pc[0:1, 0:8], -1.0), s_ve, [(s_mm, t_cs)])
    pc_free = (s_ve, t_neg)
    t_bcr = emit_bcast(b_ncr[0:1, 0:8], b_ncr_bc[:], True, (s_ve, t_neg))
    def csa(e):
        for c in range(32):
            ins = nc.tensor.matmul(pc[0:1, 0:128], c_ones_bf[:], sAal[:, c, :],
                                   start=(c == 0), stop=(c == 31))
        return ins
    t_cs2 = op("tensor", csa, s_mm, [(s_cast, t_aal), pc_free])
    t_neg2 = op("vector", lambda e: nc.vector.tensor_scalar_mul(
        b_ncr[0:1, :], pc[0:1, 0:128], -1.0), s_ve, [(s_mm, t_cs2), (s_ac, t_bcr)])
    pc_free = (s_ve, t_neg2)
    t_bca = emit_bcast(b_ncr[0:1, :], b_nca_bc, True, (s_ve, t_neg2))

    ptr_free = (s_ve, 0)
    midw_free = (s_mm, 0)
    midwT_used = 0
    for tt in range(8):
        zp = zpar
        zpar ^= 1
        def rmm(e, tt=tt, zp=zp):
            for c in range(32):
                ins = nc.tensor.matmul(pz[zp][:, 0:8], sxq[:, c, tt * 128:(tt + 1) * 128],
                                       srwt[:, c, :], start=(c == 0), stop=(c == 31))
            return ins
        t_z = op("tensor", rmm, s_mm, [pz_free[zp], (s_cast, t_rwt), (s_cast, t_sxq)])
        t_r1 = chain("vector", s_vx, s_ve, [
            lambda e, tt=tt, zp=zp: nc.vector.scalar_tensor_tensor(
                b_lg[:], b_ncr_bc[:], mo_col[:, tt:tt + 1],
                pz[zp][:, 0:8], ALU.mult, ALU.add),
            lambda e, tt=tt: nc.vector.tensor_scalar(
                b_lg[:], b_lg[:], ro_col[:, tt:tt + 1], None, ALU.mult),
            lambda e: nc.vector.reduce_max(b_m1[:], b_lg[:], axis=AX.X),
            lambda e: nc.vector.tensor_scalar_mul(b_negm1[:], b_m1[:], -1.0),
            lambda e: nc.vector.tensor_single_scalar(
                b_eq[:], b_lg[:], b_m1[:], ALU.is_equal),
            lambda e: nc.vector.scalar_tensor_tensor(
                b_msk[:], b_eq[:], -1e30, b_lg[:], ALU.mult, ALU.add),
            lambda e: nc.vector.reduce_max(b_m2[:], b_msk[:], axis=AX.X),
        ], [(s_mm, t_z), (s_ac, t_bca)])
        pz_free[zp] = (s_ve, t_r1)
        t_s1 = op("scalar", lambda e: nc.scalar.activation(
            b_s1[:], b_lg[:], AF.Exp, bias=b_negm1[:]), s_ac, [(s_ve, t_r1)])
        t_r2 = chain("vector", s_vx, s_ve, [
            lambda e: nc.vector.tensor_single_scalar(
                b_eq[:], b_lg[:], b_m2[:], ALU.is_ge),
            lambda e: nc.vector.tensor_mul(b_s1[:], b_s1[:], b_eq[:]),
            lambda e: nc.vector.reduce_sum(b_den[:], b_s1[:], axis=AX.X),
            lambda e: nc.vector.reciprocal(b_winv[:], b_den[:]),
            lambda e: nc.vector.tensor_scalar(
                b_s1[:], b_s1[:], b_winv[:], None, ALU.mult),
            lambda e, tt=tt: nc.vector.tensor_scalar(
                b_wro[:, tt, :], b_s1[:], ro_col[:, tt:tt + 1], None, ALU.mult),
        ], [(s_ac, t_s1)])
        zp = zpar
        zpar ^= 1
        def amm(e, tt=tt, zp=zp):
            for c in range(32):
                ins = nc.tensor.matmul(pz[zp][:, 0:128], sxq[:, c, tt * 128:(tt + 1) * 128],
                                       sAal[:, c, :], start=(c == 0), stop=(c == 31))
            return ins
        t_za = op("tensor", amm, s_mm, [pz_free[zp], (s_cast, t_aal)])
        def midw2(e, tt=tt):
            win = b_wro[:, tt, :].unsqueeze(2).broadcast_to([128, 8, 16])
            return nc.vector.tensor_tensor(
                b_midw[:].rearrange("p (e r) -> p e r", e=8),
                b_t1.rearrange("p (e r) -> p e r", e=8), win, ALU.mult)
        t_mw = chain("vector", s_vx, s_ve, [
            lambda e, tt=tt, zp=zp: nc.vector.scalar_tensor_tensor(
                b_t1, b_nca_bc, mo_col[:, tt:tt + 1],
                pz[zp][:, 0:128], ALU.mult, ALU.add),
            midw2,
        ], [(s_mm, t_za), (s_ve, t_r2), midw_free, ptr_free])
        pz_free[zp] = (s_ve, t_mw)
        t_tr = op("tensor", lambda e: nc.tensor.transpose(ptr[:], b_midw[:], c_id[:]),
                  s_mm, [(s_ve, t_mw), (s_ld, t_id), ptr_free])
        midw_free = (s_mm, t_tr)
        t_cpt = op("vector", lambda e, tt=tt: nc.vector.tensor_copy(
            b_midwT[:, tt * 128:(tt + 1) * 128], ptr[:]), s_ve, [(s_mm, t_tr)])
        ptr_free = (s_ve, t_cpt)
        for od in range(8):
            zp = zpar
            zpar ^= 1
            t_zi = op("tensor", lambda e, tt=tt, od=od, zp=zp: nc.tensor.matmul(
                pz[zp][:], b_midwT[:, tt * 128:(tt + 1) * 128],
                sBal[:, od * 512:(od + 1) * 512], start=True, stop=True),
                s_mm, [pz_free[zp], (s_ve, t_cpt), (s_cast, t_bal)])
            xp = x2par
            x2par ^= 1
            def ievac(e, zp=zp, xp=xp):
                t1v_ = x2buf[xp][:, 0:1024].bitcast(F32)
                return nc.vector.tensor_scalar_mul(t1v_, pz[zp][:], 2.0)
            t_ie = op("vector", ievac, s_ve, [(s_mm, t_zi), x2_free[xp]])
            pz_free[zp] = (s_ve, t_ie)
            t_fo = dma("sync", dFG[tt * 128:(tt + 1) * 128, od * 512:(od + 1) * 512],
                       x2buf[xp][:, 0:1024].bitcast(F32), s_out, [(s_ve, t_ie)])
            x2_free[xp] = (s_out, t_fo)
    midwT_used = s_mm.n
    fg_written = s_out.n
    fgt_free = [(s_ve, 0), (s_ve, 0)]
    fpar = 0
    for og in range(NOG):
        wb = og % 2
        t_wb = dma("gpsimd", wblk[wb],
                   dBWT.rearrange("(c p) o -> p c o", p=128)[:, :, og * OG:(og + 1) * OG],
                   s_cast, [wblk_free[wb]])
        def csb(e, wb=wb):
            for c in range(32):
                ins = nc.tensor.matmul(pc[0:1, 0:OG], c_ones_bf[:], wblk[wb][:, c, :],
                                       start=(c == 0), stop=(c == 31))
            return ins
        t_cs = op("tensor", csb, s_mm, [(s_cast, t_wb), pc_free])
        t_neg = op("vector", lambda e: nc.vector.tensor_scalar_mul(
            crow[:], pc[0:1, 0:OG], -1.0), s_ve, [(s_mm, t_cs), crow_free])
        pc_free = (s_ve, t_neg)
        t_cvb = emit_bcast(crow[0:1, :], cvbc[:, 0:OG], True, (s_ve, t_neg),
                           extra=[cvbc_free])
        crow_free = pc_free
        for tt in range(8):
            zp = zpar
            zpar ^= 1
            def bmm(e, wb=wb, tt=tt, zp=zp):
                for c in range(32):
                    ins = nc.tensor.matmul(pz[zp][:, 0:OG],
                                           sxq[:, c, tt * 128:(tt + 1) * 128],
                                           wblk[wb][:, c, :], start=(c == 0), stop=(c == 31))
                return ins
            t_z = op("tensor", bmm, s_mm, [pz_free[zp], (s_cast, t_wb), (s_ac, t_cvb)])
            wblk_free[wb] = (s_mm, t_z)
            fp = fpar
            fpar ^= 1
            t_fl = dma("sync", sfgt[fp], dFG[tt * 128:(tt + 1) * 128, og * OG:(og + 1) * OG],
                       s_ld, [fgt_free[fp], (s_out, fg_written)])
            xp = x2par
            x2par ^= 1
            def bevac(e, tt=tt, zp=zp, fp=fp, xp=xp):
                t1v_ = x2buf[xp][:, 0:512].bitcast(F32)
                nc.vector.scalar_tensor_tensor(t1v_, cvbc[:, 0:OG], mo_col[:, tt:tt + 1],
                                               pz[zp][:, 0:OG], ALU.mult, ALU.add)
                return nc.vector.scalar_tensor_tensor(
                    sfgt[fp], t1v_, ro_col[:, tt:tt + 1], sfgt[fp], ALU.mult, ALU.add)
            t_be = op("vector", bevac, s_ve, [(s_mm, t_z), (s_ld, t_fl), x2_free[xp]])
            pz_free[zp] = (s_ve, t_be)
            x2_free[xp] = (s_ve, t_be)
            t_fo = dma("sync", dFG[tt * 128:(tt + 1) * 128, og * OG:(og + 1) * OG],
                       sfgt[fp], s_out, [(s_ve, t_be)])
            fgt_free[fp] = (s_out, t_fo)
        cvbc_free = (s_ve, s_ve.n)
    b_last_mm = s_mm.n
    b_last_out = s_out.n

    mA = ar_bf(0, 16384).rearrange("p (n q) -> p n q", n=NTA)
    mB = ar_bf(16384, 32768).rearrange("p (n q) -> p n q", n=NTB)
    Kh = [ar_bf(49152, 8192), ar_bf(57344, 8192)]
    Vh = [wblk2[:, 0:32, :].rearrange("p a b -> p (a b)").rearrange(
              "p (n d) -> p n d", d=128),
          wblk2[:, 32:64, :].rearrange("p a b -> p (a b)").rearrange(
              "p (n d) -> p n d", d=128)]
    qh = [kaff[0][:], kaff[1][:]]
    PT = ([mbc[:, i * 512:(i + 1) * 512] for i in range(4)]
          + [cosbuf[:, 0:512], cosbuf[:, 512:1024], cosbuf[:, 1024:1536],
             cosbuf[:, 1536:2048]])
    linv = x2buf[0][:, 0:1024].bitcast(F32)
    lbc = [rbc[:, 0:512], rbc[:, 512:1024]]
    aout = [ksh[0][:, 0:512], ksh[0][:, 512:1024]]

    t_mA = dma("sync", mA, dMA, s_ld, [(s_mm, b_last_mm), (s_out, b_last_out)])
    t_mB = dma("sync", mB, dMB, s_ld)
    vg_r = dVG.rearrange("(n p) d -> p n d", p=128)

    c_start_ve = (s_ve, s_ve.n)
    c_start_ld = (s_ld, s_ld.n)
    kh_free = [c_start_ve, c_start_ve]
    vh_free = [(s_mm, b_last_mm), (s_mm, b_last_mm)]
    qh_free = [kaff_free[0], kaff_free[1]]
    pt_free = [c_start_ve] * 8
    ps_free = {id(pz[0]): pz_free[0], id(pz[1]): pz_free[1],
               id(pst[0]): c_start_ve, id(pst[1]): c_start_ve,
               id(pst[2]): c_start_ve, id(pst[3]): c_start_ve}
    po_free = [c_start_ve, c_start_ve]
    pl_free = [c_start_ve, c_start_ve]
    lbc_free = [c_start_ve, c_start_ve]
    aout_free = [ksh_free[0], ksh_free[0]]

    kload = [0, 0]
    vload = [0, 0]
    qload = [0, 0]
    sctr = 0
    ptctr = 0
    ints_written = (s_out, s_out.n)
    for h in range(H):
        par = h % 2
        kload[par] = dma("sync", Kh[par], dKG[h * 128:(h + 1) * 128, :], s_ld,
                         [kh_free[par]] + ([ints_written] if h == 0 else []))
        qload[par] = dma("sync", qh[par], dQG[h * 128:(h + 1) * 128, :], s_ld,
                         [qh_free[par]])
        for j4 in range(4):
            vload[par] = dma("scalar", Vh[par][:, j4 * 16:(j4 + 1) * 16, :],
                             vg_r[:, j4 * 16:(j4 + 1) * 16, h * 128:(h + 1) * 128],
                             s_sh, ([vh_free[par]] + ([ints_written] if h == 0 else []))
                             if j4 == 0 else [])
        for (cidx, ntiles, qoff, po, pl, msk) in (
                (0, NTA, 0, 0, 2, mA), (1, NTB, 512, 1, 3, mB)):
            sbanks = [pz[0], pz[1], pst[1 - po], pst[2 + (1 - po)]]
            DELAY = 2
            pend = {}
            for j in range(ntiles + DELAY):
                if j < ntiles:
                    sb = sbanks[j % 4]
                    ptp = ptctr % 8
                    ptctr += 1
                    t_smm = op("tensor",
                               lambda e, par=par, j=j, qoff=qoff, sb=sb:
                               nc.tensor.matmul(sb[:], Kh[par][:, j * 128:(j + 1) * 128],
                                                qh[par][:, qoff:qoff + 512],
                                                start=True, stop=True),
                               s_mm, [ps_free[id(sb)], (s_ld, kload[par]),
                                      (s_ld, qload[par])])
                    t_exp = op("scalar", lambda e, sb=sb, ptp=ptp: nc.scalar.activation(
                        PT[ptp], sb[:], AF.Exp, bias=c_negC[:], scale=SM_SCALE),
                        s_ac, [(s_mm, t_smm), pt_free[ptp], (s_gp, t_consts)])
                    ps_free[id(sb)] = (s_ac, t_exp)
                    t_msk = op("vector", lambda e, ptp=ptp, msk=msk, j=j:
                               nc.vector.tensor_mul(PT[ptp], PT[ptp], msk[:, j, :]),
                               s_ve, [(s_ac, t_exp), (s_ld, t_mB)])
                    pend[j] = (ptp, t_msk)
                if j >= DELAY:
                    jj = j - DELAY
                    ptp2, t_msk2 = pend.pop(jj)
                    def lpv(e, par=par, jj=jj, po=po, ptp=ptp2,
                            st=(jj == 0), sp_=(jj == ntiles - 1)):
                        nc.tensor.matmul(pst[2 + po][0:1, :], c_ones_bf[:], PT[ptp],
                                         start=st, stop=sp_)
                        return nc.tensor.matmul(pst[po][:], Vh[par][:, jj, :], PT[ptp],
                                                start=st, stop=sp_)
                    w = [(s_ve, t_msk2), (s_sh, vload[par])]
                    if jj == 0:
                        w += [po_free[po], pl_free[po],
                              ps_free[id(pst[po])], ps_free[id(pst[2 + po])]]
                    t_lpv = op("tensor", lpv, s_mm, w)
                    pt_free[ptp2] = (s_mm, t_lpv)
            t_rec = op("vector", lambda e, pl=pl: nc.vector.reciprocal(
                linv[0:1, :], pst[pl][0:1, :]), s_ve, [(s_mm, t_lpv)])
            t_bc = op("tensor", lambda e: nc.tensor.matmul(
                pc[:], c_ones1[:], linv[0:1, :], start=True, stop=True),
                s_mm, [(s_ve, t_rec), pc_free])
            t_cp = op("scalar", lambda e, cidx=cidx: nc.scalar.copy(
                lbc[cidx], pc[:]), s_ac, [(s_mm, t_bc), lbc_free[cidx]])
            pc_free = (s_ac, t_cp)
            def fin(e, po=po, cidx=cidx):
                return nc.vector.tensor_mul(aout[cidx], pst[po][:], lbc[cidx])
            t_fin = op("vector", fin, s_ve,
                       [(s_ac, t_cp), aout_free[cidx]])
            po_free[po] = (s_ve, t_fin)
            pl_free[po] = (s_ve, t_rec)
            ps_free[id(pst[po])] = (s_ve, t_fin)
            ps_free[id(pst[2 + po])] = (s_ve, t_rec)
            lbc_free[cidx] = (s_ve, t_fin)
            t_ao = dma("sync", dAG[h * 128:(h + 1) * 128, qoff:qoff + 512],
                       aout[cidx], s_out, [(s_ve, t_fin)])
            aout_free[cidx] = (s_out, t_ao)
        kh_free[par] = (s_mm, s_mm.n)
        vh_free[par] = (s_mm, s_mm.n)
        qh_free[par] = (s_mm, s_mm.n)
    c_last_mm = s_mm.n
    c_last_gp = s_gp.n
    c_last_out = s_out.n

    sag = arena[:, 0:32768].rearrange("p (c t) -> p c t", c=32)
    woblk = [arena[:, 32768:49152].rearrange("p (c o) -> p c o", c=32),
             arena[:, 49152:65536].rearrange("p (c o) -> p c o", c=32)]
    sxot = [x2buf[0][:, 0:1024].bitcast(F32), x2buf[1][:, 0:1024].bitcast(F32)]
    sfgt2 = [rbc[:, 0:512], rbc[:, 512:1024]]
    syb = [rbc[:, 1024:1536], rbc[:, 1536:2048]]

    t_ag = dma("sync", sag, dAG.rearrange("(c p) t -> p c t", p=128), s_ld,
               [(s_gp, c_last_gp), (s_mm, c_last_mm), (s_out, c_last_out)])
    wo_free = [(s_mm, c_last_mm), (s_mm, c_last_mm)]
    xot_free = [(s_mm, c_last_mm), (s_mm, c_last_mm)]
    fgt2_free = [(s_ve, s_ve.n), (s_ve, s_ve.n)]
    yb_free = [(s_st, 0), (s_st, 0)]
    dpar = 0
    for og in range(8):
        wb = og % 2
        t_wb = dma("gpsimd", woblk[wb],
                   dWO.rearrange("(c p) o -> p c o", p=128)[:, :, og * 512:(og + 1) * 512],
                   s_cast, [wo_free[wb]])
        for tt in range(8):
            zp = zpar
            zpar ^= 1
            def dmm(e, wb=wb, tt=tt, zp=zp):
                for c in range(32):
                    ins = nc.tensor.matmul(pz[zp][:], sag[:, c, tt * 128:(tt + 1) * 128],
                                           woblk[wb][:, c, :], start=(c == 0), stop=(c == 31))
                return ins
            t_z = op("tensor", dmm, s_mm,
                     [ps_free[id(pz[zp])], (s_cast, t_wb), (s_ld, t_ag)])
            wo_free[wb] = (s_mm, t_z)
            dp = dpar
            dpar ^= 1
            t_xl = dma("sync", sxot[dp][:, 0:512],
                       dXO[tt * 128:(tt + 1) * 128, og * 512:(og + 1) * 512], s_ld,
                       [xot_free[dp]])
            t_fl = dma("sync", sfgt2[dp],
                       dFG[tt * 128:(tt + 1) * 128, og * 512:(og + 1) * 512], s_ld,
                       [fgt2_free[dp]])
            def devac(e, zp=zp, dp=dp):
                nc.vector.tensor_add(syb[dp], pz[zp][:], sfgt2[dp])
                return nc.vector.tensor_add(syb[dp], syb[dp], sxot[dp][:, 0:512])
            t_de = op("vector", devac, s_ve,
                      [(s_mm, t_z), (s_ld, t_xl), (s_ld, t_fl), yb_free[dp]])
            ps_free[id(pz[zp])] = (s_ve, t_de)
            xot_free[dp] = (s_ve, t_de)
            fgt2_free[dp] = (s_ve, t_de)
            t_y = dma("sync", dY[tt * 128:(tt + 1) * 128, og * 512:(og + 1) * 512],
                      syb[dp], s_st, [(s_ve, t_de)])
            yb_free[dp] = (s_st, t_y)

    def final_wait(e):
        e.wait_ge(s_st.h, s_st.n)
    emit("sync", final_wait)

    with nc.Block() as block:
        @block.tensor
        def _(eng):
            for fn in Q["tensor"]:
                fn(eng)

        @block.vector
        def _(eng):
            for fn in Q["vector"]:
                fn(eng)

        @block.scalar
        def _(eng):
            for fn in Q["scalar"]:
                fn(eng)

        @block.gpsimd
        def _(eng):
            for fn in Q["gpsimd"]:
                fn(eng)

        @block.sync
        def _(eng):
            for fn in Q["sync"]:
                fn(eng)

    return nc


def _rope_tables(pos):
    d = np.arange(128)
    inv = 10000.0 ** (-(2.0 * (d % 64)) / 128.0)
    ang = inv[:, None] * pos[None, :].astype(np.float64)
    cos = np.cos(ang).astype(np.float32)
    sin = np.sin(ang).astype(np.float32)
    sin[0:64] *= -1.0
    return cos.astype(ml_dtypes.bfloat16), sin.astype(ml_dtypes.bfloat16)


def _masks(core):
    cA, cB = core, 15 - core
    ql = np.arange(512)
    p = np.arange(128)
    mA = np.zeros((128, NTA, 512), dtype=np.float32)
    mB = np.zeros((128, NTB, 512), dtype=np.float32)
    for j in range(NTA):
        kg = 128 * j + p
        mA[:, j, :] = (kg[:, None] <= (512 * cA + ql)[None, :])
    for j in range(NTB):
        kg = 128 * j + p
        mB[:, j, :] = (kg[:, None] <= (512 * cB + ql)[None, :])
    return mA.astype(ml_dtypes.bfloat16), mB.astype(ml_dtypes.bfloat16)


_NC_CACHE = {}


def kernel(L0_attn_hidden_states, L0_attn_wq, L0_attn_wk, L0_attn_wv, L0_attn_wo,
           L0_attn_kq1, L0_attn_kq2, L0_attn_kk1, L0_attn_kk2, L0_attn_scalingD,
           L0_attn_interaction_k, L0_attn_scalingD2, L0_ffn_base_W, L0_ffn_router_W,
           L0_ffn_lora_A, L0_ffn_lora_B):
    x = np.ascontiguousarray(np.asarray(L0_attn_hidden_states, dtype=np.float32)[0])
    xT = np.ascontiguousarray(x.T)
    wq = np.ascontiguousarray(L0_attn_wq, dtype=np.float32)
    wk = np.ascontiguousarray(L0_attn_wk, dtype=np.float32)
    wv = np.ascontiguousarray(L0_attn_wv, dtype=np.float32)
    wo = np.ascontiguousarray(L0_attn_wo, dtype=np.float32)
    bWT = np.ascontiguousarray(np.asarray(L0_ffn_base_W, dtype=np.float32).T)
    rWT = np.ascontiguousarray(np.asarray(L0_ffn_router_W, dtype=np.float32).T)
    Aall = np.ascontiguousarray(
        np.asarray(L0_ffn_lora_A, dtype=np.float32).transpose(1, 0, 2).reshape(D, 128))
    Ball = np.ascontiguousarray(
        np.asarray(L0_ffn_lora_B, dtype=np.float32).reshape(128, D))
    ident = np.eye(128, dtype=ml_dtypes.bfloat16)
    cosk, sink = _rope_tables(np.arange(S))

    if "nc" not in _NC_CACHE:
        _NC_CACHE["nc"] = build()
    nc = _NC_CACHE["nc"]

    in_maps = []
    for i in range(NCORE):
        cA, cB = i, 15 - i
        rows = np.r_[cA * 512:(cA + 1) * 512, cB * 512:(cB + 1) * 512]
        xo = np.ascontiguousarray(x[rows])
        xoT = np.ascontiguousarray(xT[:, rows])
        cosq, sinq = _rope_tables(rows)
        mA, mB = _masks(i)
        in_maps.append({
            "xT": xT, "xo": xo, "xoT": xoT,
            "wq": wq, "wk": wk, "wv": wv, "wo": wo,
            "bWT": bWT, "rWT": rWT, "Aall": Aall, "Ball": Ball,
            "cosq": cosq, "sinq": sinq, "cosk": cosk, "sink": sink,
            "maskA": mA, "maskB": mB, "ident": ident,
        })
    res = run_bass_kernel_spmd(nc, in_maps, core_ids=list(range(NCORE)))
    y = np.empty((S, D), dtype=np.float32)
    for i in range(NCORE):
        yi = np.asarray(res.results[i]["y"])
        y[i * 512:(i + 1) * 512] = yi[0:512]
        y[(15 - i) * 512:(16 - i) * 512] = yi[512:1024]
    return y.reshape(1, S, D)
